# version 04
# baseline (speedup 1.0000x reference)
import dataclasses
import os
import numpy as np
import ml_dtypes
from contextlib import ExitStack

import concourse.bass as bass
import concourse.mybir as mybir
from concourse.bass_utils import run_bass_kernel_spmd

F32 = mybir.dt.float32
BF16 = mybir.dt.bfloat16
AF_ = mybir.ActivationFunctionType
ALU = mybir.AluOpType
AX = mybir.AxisListType
BF_NP = ml_dtypes.bfloat16

B, T, C, H, DH = 8, 2048, 1024, 16, 64
NCH = 8
NQB = 4
QBW = 512
E, HF = 64, 256
ES = 2
EPS = 1e-5
VPS = 3
NSLOT = 3

ENGINES = ("sync", "gpsimd", "tensor", "vector", "scalar")


class Ctr:
    def __init__(self, sem, step):
        self.sem = sem
        self.n = 0
        self.step = step


class Emitter:
    def __init__(self):
        self.q = {e: [] for e in ENGINES}

    def op(self, eng, fn, waits=(), inc=None):
        w = []
        def add(x):
            if x is None:
                return
            if isinstance(x, list):
                for y in x:
                    add(y)
            elif isinstance(x, Ctr):
                w.append((x.sem, x.n))
            else:
                w.append(x)
        for x in waits:
            add(x)
        ev = None
        if inc is not None:
            inc.n += inc.step
            ev = inc if inc.step == 16 else (inc.sem, inc.n)
        self.q[eng].append((w, fn, inc))
        return ev

    def runner(self, name):
        ops = self.q[name]

        def run(eng):
            waited = {}
            for w, fn, inc in ops:
                for sem, val in w:
                    k = id(sem)
                    if waited.get(k, -1) < val:
                        eng.wait_ge(sem, val)
                        waited[k] = val
                inst = fn(eng)
                if inc is not None:
                    inst.then_inc(inc.sem, inc.step)
        return run


def _gammas():
    return 1.0 - 2.0 ** (-5.0 - np.arange(H, dtype=np.float64))


def _rope_tables():
    inv_freq = (1.0 / (10000.0 ** (np.arange(0, DH, 2, dtype=np.float32)
                                   / np.float32(DH)))).astype(np.float32)
    t = np.arange(T, dtype=np.float32)
    freqs = (t[:, None] * inv_freq[None, :]).astype(np.float32)
    emb = np.concatenate([freqs, freqs], axis=-1)
    cos2 = np.concatenate([np.cos(emb).T] * 2, axis=0).astype(np.float32)
    sin2 = np.concatenate([np.sin(emb).T] * 2, axis=0).astype(np.float32)
    return cos2, sin2


def build_program(taps=(), phase_limit="all"):
    nc = bass.Bass("TRN2", target_bir_lowering=False,
                   detect_race_conditions=False)
    root = ExitStack()

    def din(name, shape):
        return nc.dram_tensor(name, list(shape), F32, kind="ExternalInput").ap()

    xfm = din("xfm", (C, T))
    wT = {"q": din("wqT", (C, C)), "k": din("wkT", (C, C)),
          "v": din("wvT", (C, C)), "o": din("woT", (C, C))}
    gnw8 = din("gnw8", (128, NCH)); gnb8 = din("gnb8", (128, NCH))
    rw = din("rw", (C, E)); rb = din("rb", (1, E))
    w1 = din("w1", (E, C, HF)); b1L = din("b1L", (128, 2 * E))
    w2 = din("w2", (E, HF, C)); b2d = din("b2d", (E, C))
    sw1h = din("sw1h", (ES, 2, C, 1024)); sb1L = din("sb1L", (128, 32))
    sw2 = din("sw2", (ES, 2048, C)); sb2d = din("sb2d", (ES, C))

    out_fm = nc.dram_tensor("out_fm", [C, T], F32, kind="ExternalOutput").ap()
    x2_hbm = nc.dram_tensor("x2_hbm", [C, T], F32, kind="Internal").ap()
    xln2_hbm = nc.dram_tensor("xln2_hbm", [C, T], BF16, kind="Internal").ap()

    tap_out = {}
    tap_shapes = {"xln1": (C, T), "q": (C, T), "k": (C, T), "v": (T, C),
                  "attnraw": (C, T), "attnN": (C, T), "af": (C, T),
                  "x2": (C, T), "xln2": (C, T), "cwem": (E, T),
                  "racc": (C, T), "cwun": (128, 1024), "rdbg": (128, 48)}
    for tp in taps:
        tap_out[tp] = nc.dram_tensor("tap_" + tp, list(tap_shapes[tp]), F32,
                                     kind="ExternalOutput").ap()

    gam = _gammas()
    cos2, sin2 = _rope_tables()
    c_cos = nc.inline_tensor(cos2.astype(BF_NP), name="c_cos").ap()
    c_sin = nc.inline_tensor(sin2.astype(BF_NP), name="c_sin").ap()
    Rr = np.zeros((DH, DH), dtype=np.float32)
    for dd in range(32):
        Rr[dd, dd + 32] = -1.0
        Rr[dd + 32, dd] = 1.0
    R2 = np.zeros((128, 128), dtype=np.float32)
    R2[:64, :64] = Rr; R2[64:, 64:] = Rr
    c_R2T = nc.inline_tensor(np.ascontiguousarray(R2.T), name="c_R2T").ap()
    c_ident = nc.inline_tensor(np.eye(128, dtype=np.float32), name="c_ident").ap()
    c_ones1 = nc.inline_tensor(np.ones((1, 128), np.float32), name="c_ones1").ap()
    c_ones2c = nc.inline_tensor(np.ones((128, 2), np.float32), name="c_ones2c").ap()
    c_ones22 = nc.inline_tensor(np.ones((2, 2), np.float32), name="c_ones22").ap()
    cz = np.ones((128, 512), np.float32)
    for jk in range(128):
        cz[jk, 0:jk] = 0.0
    c_causal = nc.inline_tensor(cz, name="c_causal").ap()
    ps_all = np.zeros((128, 19 * H), np.float64)
    jj_ = np.arange(128, dtype=np.float64)
    for h in range(H):
        for ccx in range(19):
            ps_all[:, 19 * h + ccx] = gam[h] ** ((1920.0 - 128.0 * ccx) - jj_) / 8.0
    c_pscale = nc.inline_tensor(ps_all.astype(np.float32), name="c_pscale").ap()
    g2 = np.zeros((128, NCH * QBW), np.float64)
    jq_ = np.arange(QBW, dtype=np.float64)
    for pr in range(NCH):
        g2[:64, pr * QBW:(pr + 1) * QBW] = gam[2 * pr] ** jq_
        g2[64:, pr * QBW:(pr + 1) * QBW] = gam[2 * pr + 1] ** jq_
    c_G2 = nc.inline_tensor(g2.astype(np.float32).astype(BF_NP), name="c_G2").ap()
    selv = np.zeros((128, 64 * NCH), np.float32)
    selq = np.zeros((128, 64 * NCH), np.float32)
    for pr in range(NCH):
        selv[:64, 64 * pr + 2 * pr] = 1.0
        selv[64:, 64 * pr + 2 * pr + 1] = 1.0
        selq[:64, 64 * pr + 32 + 2 * pr] = 1.0
        selq[64:, 64 * pr + 32 + 2 * pr + 1] = 1.0
    c_selv = nc.inline_tensor(selv.astype(BF_NP), name="c_selv").ap()
    c_selq = nc.inline_tensor(selq.astype(BF_NP), name="c_selq").ap()
    selp = np.zeros((16, 128 * NCH), np.float32)
    for pr in range(NCH):
        selp[2 * pr, 128 * pr:128 * pr + 64] = 1.0
        selp[2 * pr + 1, 128 * pr + 64:128 * pr + 128] = 1.0
    c_selp = nc.inline_tensor(selp, name="c_selp").ap()
    rowsel = np.zeros((8, 8 * 128), np.float32)
    for r8 in range(8):
        rowsel[r8, 128 * r8:128 * (r8 + 1)] = 1.0
    c_rowsel = nc.inline_tensor(rowsel, name="c_rowsel").ap()
    c_id64 = nc.inline_tensor(np.eye(64, dtype=np.float32).astype(BF_NP),
                              name="c_id64").ap()

    def bcast_col(ap_col, n):
        return dataclasses.replace(
            ap_col, ap=[[ap_col.ap[0][0], ap_col.shape[0]], [0, n]])

    dS = Ctr(root.enter_context(nc.semaphore("dS")), 16)
    dG = Ctr(root.enter_context(nc.semaphore("dG")), 16)
    dW0 = Ctr(root.enter_context(nc.semaphore("dW0")), 16)
    dW1 = Ctr(root.enter_context(nc.semaphore("dW1")), 16)
    dW2 = Ctr(root.enter_context(nc.semaphore("dW2")), 16)
    dWs = [dW0, dW1, dW2]
    cT = Ctr(root.enter_context(nc.semaphore("cT")), 1)
    cV = Ctr(root.enter_context(nc.semaphore("cV")), 1)
    cS = Ctr(root.enter_context(nc.semaphore("cS")), 1)

    EM = Emitter()

    def finish(*stacks):
        EM.op("sync", lambda e: e.nop(),
              waits=[dS, dG, dW0, dW1, dW2])
        with nc.Block() as block:
            block.sync(EM.runner("sync"))
            block.gpsimd(EM.runner("gpsimd"))
            block.tensor(EM.runner("tensor"))
            block.vector(EM.runner("vector"))
            block.scalar(EM.runner("scalar"))
        for s_ in stacks:
            s_.close()
        root.close()
        return nc

    def allevs():
        return [dS, dG, dW0, dW1, dW2,
                (cT.sem, cT.n), (cV.sem, cV.n), (cS.sem, cS.n)]

    def sb(stack, name, shape, dtype):
        return stack.enter_context(nc.sbuf_tensor(name, list(shape), dtype))

    gnw_sb = sb(root, "gnw_sb", [128, NCH], F32)
    gnb_sb = sb(root, "gnb_sb", [128, NCH], F32)
    ones1_sb = sb(root, "ones1_sb", [1, 128], F32)
    ones2c_sb = sb(root, "ones2c_sb", [128, 2], F32)
    ones22_sb = sb(root, "ones22_sb", [2, 2], F32)
    ident_sb = sb(root, "ident_sb", [128, 128], F32)
    lnr_sb = sb(root, "lnr_sb", [1, 1536], F32)

    pb = [root.enter_context(nc.psum_tensor("pbk%d" % i, [128, 512], F32))
          for i in range(8)]

    for dst, src in ((gnw_sb, gnw8), (gnb_sb, gnb8), (ones1_sb, c_ones1),
                     (ones2c_sb, c_ones2c), (ones22_sb, c_ones22),
                     (ident_sb, c_ident)):
        EM.op("sync", lambda e, d=dst, s=src: e.dma_start(out=d[:, :], in_=s[:, :]),
              inc=dS)
    ev_glob = dS

    st_ = {"pb4": None, "pb5": None, "pb6": None, "pb7": None,
           "fin": None, "rows": None}

    def ln_block(src, ev_src, xsq_p, ev_xsqf):
        ev_sq = {}
        ev_mm = None
        for cc in range(NCH):
            sqb = xsq_p[cc % 2]
            w = [ev_src(cc), ev_xsqf[cc % 2]]
            ev_sq[cc] = EM.op("scalar",
                              lambda e, d=sqb, s=src(cc): e.activation(
                                  d[:, :], s, AF_.Square),
                              waits=w, inc=cS)
            first = (cc == 0)
            wv = [ev_src(cc), ev_glob]
            if first:
                wv.append(st_["pb6"])
            EM.op("tensor",
                  lambda e, s=src(cc), f=first, sp2=(cc == NCH - 1): e.matmul(
                      pb[6][0:2, :], ones2c_sb[:, :], s,
                      start=f, stop=sp2), waits=wv)
            sp = (cc == NCH - 1)
            wq = [ev_sq[cc]]
            if first:
                wq.append(st_["pb7"])
            ev = EM.op("tensor",
                       lambda e, d=sqb, f=first, sp=sp: e.matmul(
                           pb[7][0:2, :], ones2c_sb[:, :], d[:, :],
                           start=f, stop=sp),
                       waits=wq, inc=cT)
            ev_xsqf[cc % 2] = ev
            if sp:
                ev_mm = ev
        A, Bc, Cc = lnr_sb[:, 0:512], lnr_sb[:, 512:1024], lnr_sb[:, 1024:1536]
        ev1 = EM.op("vector", lambda e: e.tensor_scalar(
            A, pb[6][0:1, :], 1.0 / C, None, ALU.mult),
            waits=[ev_mm, st_["fin"]], inc=cV)
        ev2 = EM.op("vector", lambda e: e.tensor_scalar(
            Bc, pb[7][0:1, :], 1.0 / C, EPS, ALU.mult, ALU.add),
            waits=[ev_mm], inc=cV)
        st_["pb6"] = ev1
        st_["pb7"] = ev2
        ev3 = EM.op("vector", lambda e: e.scalar_tensor_tensor(
            Cc, A, -1.0, A, ALU.mult, ALU.mult), waits=[ev1], inc=cV)
        ev4 = EM.op("vector", lambda e: e.tensor_tensor(
            Bc, Bc, Cc, ALU.add), waits=[ev2, ev3], inc=cV)
        ev5 = EM.op("vector", lambda e: e.reciprocal(Cc, Bc),
                    waits=[ev4], inc=cV)
        ev6 = EM.op("scalar", lambda e: e.activation(Bc, Cc, AF_.Sqrt),
                    waits=[ev5], inc=cS)
        ev7 = EM.op("vector", lambda e: e.scalar_tensor_tensor(
            A, A, -1.0, Bc, ALU.mult, ALU.mult), waits=[ev6], inc=cV)
        st_["fin"] = ev7
        EM.op("tensor", lambda e: e.matmul(
            pb[4][:, :], ones1_sb[:, :], Bc, start=True, stop=True),
            waits=[ev6, ev7, st_["pb4"], st_["pb5"]])
        evb = EM.op("tensor", lambda e: e.matmul(
            pb[5][:, :], ones1_sb[:, :], A, start=True, stop=True),
            waits=[ev7], inc=cT)
        return evb

    attn = ExitStack()
    q_bf = sb(attn, "q_bf", [128, NCH * T], BF16)
    k_bf = sb(attn, "k_bf", [128, NCH * T], BF16)
    v_bf = sb(attn, "v_bf", [128, 16 * 1024], BF16)
    xt_one = sb(attn, "xtA", [128, NCH * 512], F32)
    xt2 = [xt_one, xt_one]
    xsq = [sb(attn, "xsq0", [128, 512], F32), sb(attn, "xsq1", [128, 512], F32)]
    ttmp = sb(attn, "ttmp", [128, 512], F32)
    wbuf2 = [sb(attn, "wbufA", [128, NCH * 1024], BF16),
             sb(attn, "wbufB", [128, NCH * 1024], BF16)]
    xl2st = [sb(attn, "xl2st0", [128, 512], BF16),
             sb(attn, "xl2st1", [128, 512], BF16)]
    p12 = ExitStack()
    xln1_bf = sb(p12, "xln1_bf", [128, NCH * T], BF16)
    cos_sb = sb(p12, "cos_sb", [128, T], BF16)
    sin_sb = sb(p12, "sin_sb", [128, T], BF16)
    r2t_sb = sb(p12, "r2t_sb", [128, 128], F32)

    for dst, src in ((cos_sb, c_cos), (sin_sb, c_sin), (r2t_sb, c_R2T)):
        EM.op("sync", lambda e, d=dst, s=src: e.dma_start(out=d[:, :], in_=s[:, :]),
              inc=dS)
    ev_tab = dS

    ev_xsqf = [None, None]
    ev_xt = {}
    ev_xt_free = [None, None]
    ev_xln1_blk = [None] * NQB
    for blk in range(NQB):
        xt = xt2[blk % 2]
        for cc in range(NCH):
            ev_xt[cc] = EM.op("sync", lambda e, c=cc, b=blk, x_=xt:
                              e.dma_start(
                out=x_[:, 512 * c:512 * (c + 1)],
                in_=xfm[128 * c:128 * (c + 1), 512 * b:512 * (b + 1)]),
                waits=[ev_xt_free[0], ev_xt_free[1]], inc=dS)
        evb = ln_block(lambda cc: xt[:, 512 * cc:512 * (cc + 1)],
                       lambda cc: ev_xt[cc], xsq, ev_xsqf)
        evtt = None
        for cc in range(NCH):
            EM.op("vector", lambda e, c=cc, x_=xt: e.tensor_tensor(
                ttmp[:, :], x_[:, 512 * c:512 * (c + 1)], pb[4][:, :],
                ALU.mult), waits=[evb])
            evtt = EM.op("vector", lambda e, c=cc, b=blk: e.tensor_tensor(
                xln1_bf[:, c * T + 512 * b:c * T + 512 * (b + 1)],
                ttmp[:, :], pb[5][:, :], ALU.add), inc=cV)
        ev_xt_free[blk % 2] = evtt
        st_["pb4"] = evtt
        st_["pb5"] = evtt
        ev_xln1_blk[blk] = evtt

    if "xln1" in tap_out:
        for cc in range(NCH):
            EM.op("gpsimd", lambda e, c=cc: e.dma_start(
                out=tap_out["xln1"][128 * c:128 * (c + 1), :],
                in_=xln1_bf[:, c * T:(c + 1) * T]), waits=allevs(), inc=dG)
    if phase_limit == "p1":
        return finish(p12, attn)

    wslot_i = [0]

    def w_dma(which, ev_free):
        buf = wbuf2[wslot_i[0] % 2]
        wslot_i[0] += 1
        last = None
        for piece in range(4):
            w = [ev_free] if piece == 0 else []
            last = EM.op("gpsimd", lambda e, p=piece, t=wT[which], b=buf:
                         e.dma_start(
                out=b[:, 2048 * p:2048 * (p + 1)].rearrange(
                    "p (a o) -> p a o", a=2),
                in_=t[256 * p:256 * (p + 1), :].rearrange(
                    "(a p) o -> p a o", p=128)), waits=w, inc=dG)
        return last, buf

    ev_wv, wbuf = w_dma("v", None)
    ev_vps = {0: None, 1: None, 2: None, 3: None}
    ev_v_done = None
    for i in range(16):
        blk = i // 4
        pp = (i % 2) * 2
        for oh in range(2):
            bank = pp + oh
            w = [ev_wv, ev_xln1_blk[blk], ev_vps[bank]]
            evmm = None
            for cc in range(NCH):
                evmm = EM.op("tensor", lambda e, c=cc, ii=i, o=oh, bk=bank,
                             wb=wbuf: e.matmul(
                    pb[bk][:, :],
                    xln1_bf[:, c * T + 128 * ii:c * T + 128 * (ii + 1)],
                    wb[:, c * 1024 + o * 512:c * 1024 + (o + 1) * 512],
                    start=(c == 0), stop=(c == NCH - 1)),
                    waits=(w if cc == 0 else []),
                    inc=(cT if cc == NCH - 1 else None))
            ev_vps[bank] = EM.op("scalar", lambda e, ii=i, o=oh, bk=bank: e.copy(
                v_bf[:, ii * 1024 + o * 512:ii * 1024 + (o + 1) * 512],
                pb[bk][:, :]), waits=[evmm], inc=cS)
            ev_v_done = ev_vps[bank]

    if "v" in tap_out:
        for i in range(16):
            EM.op("gpsimd", lambda e, ii=i: e.dma_start(
                out=tap_out["v"][128 * ii:128 * (ii + 1), :],
                in_=v_bf[:, ii * 1024:(ii + 1) * 1024]), waits=allevs(), inc=dG)

    qk_dst = {"q": q_bf, "k": k_bf}
    ev_ps_free = {0: ev_vps[0], 1: ev_vps[1], 2: ev_vps[2], 3: ev_vps[3]}
    ev_qtmp = [[ev_xt_free[0]], [ev_xt_free[0]]]
    ev_qk_done = None
    buf_free = {0: ev_v_done, 1: None}
    for which in ("q", "k"):
        ev_w, wbuf = w_dma(which, buf_free[wslot_i[0] % 2])
        dst = qk_dst[which]
        for oc in range(NCH):
            for blk in range(NQB):
                par = (oc * NQB + blk) % 2
                w = [ev_w, ev_xln1_blk[blk], ev_tab, ev_ps_free[par]]
                evmm = None
                for cc in range(NCH):
                    evmm = EM.op("tensor", lambda e, c=cc, o=oc, b=blk, p=par,
                                 wb=wbuf: e.matmul(
                        pb[p][:, :],
                        wb[:, c * 1024 + o * 128:c * 1024 + o * 128 + 128],
                        xln1_bf[:, c * T + 512 * b:c * T + 512 * (b + 1)],
                        start=(c == 0), stop=(c == NCH - 1)),
                        waits=(w if cc == 0 else []),
                        inc=(cT if cc == NCH - 1 else None))
                evq = EM.op("scalar", lambda e, p=par: e.copy(
                    xt2[0][:, 512 * p:512 * (p + 1)], pb[p][:, :]),
                    waits=[evmm] + ev_qtmp[par], inc=cS)
                ev_ps_free[par] = evq
                evr = EM.op("tensor", lambda e, p=par: e.matmul(
                    pb[2 + p][:, :], r2t_sb[:, :],
                    xt2[0][:, 512 * p:512 * (p + 1)],
                    start=True, stop=True),
                    waits=[evq, ev_ps_free[2 + par]], inc=cT)
                evt1 = EM.op("vector", lambda e, p=par, b=blk: e.tensor_tensor(
                    xt2[0][:, 1024 + 512 * p:1024 + 512 * (p + 1)],
                    xt2[0][:, 512 * p:512 * (p + 1)],
                    cos_sb[:, 512 * b:512 * (b + 1)], ALU.mult),
                    waits=[evq], inc=cV)
                evt2 = EM.op("vector", lambda e, p=par, b=blk: e.tensor_tensor(
                    xt2[0][:, 2048 + 512 * p:2048 + 512 * (p + 1)],
                    pb[2 + p][:, :], sin_sb[:, 512 * b:512 * (b + 1)],
                    ALU.mult), waits=[evr], inc=cV)
                ev_ps_free[2 + par] = evt2
                evo = EM.op("vector", lambda e, p=par, o=oc, b=blk, d=dst:
                            e.tensor_tensor(
                    d[:, o * T + 512 * b:o * T + 512 * (b + 1)],
                    xt2[0][:, 1024 + 512 * p:1024 + 512 * (p + 1)],
                    xt2[0][:, 2048 + 512 * p:2048 + 512 * (p + 1)], ALU.add),
                    inc=cV)
                ev_qtmp[par] = [evr, evt1, evo]
                ev_qk_done = evo
        buf_free[1 - wslot_i[0] % 2] = evmm

    for tp, buf in (("q", q_bf), ("k", k_bf)):
        if tp in tap_out:
            for cc in range(NCH):
                EM.op("gpsimd", lambda e, c=cc, b=buf, t_=tp: e.dma_start(
                    out=tap_out[t_][128 * c:128 * (c + 1), :],
                    in_=b[:, c * T:(c + 1) * T]), waits=allevs(), inc=dG)

    if phase_limit == "p2":
        return finish(p12, attn)

    p12_bar = allevs()
    p12.close()
    af = sb(attn, "af", [128, NCH * 512], F32)
    scob = [sb(attn, "scob%d" % i, [128, 512], BF16) for i in range(4)]
    attnraw = sb(attn, "attnraw", [128, NCH * 512], BF16)
    attnN = attnraw
    asq = [sb(attn, "asq0", [128, 512], BF16), sb(attn, "asq1", [128, 512], BF16)]
    g2_sb = [sb(attn, "g2a", [128, 512], BF16), sb(attn, "g2b", [128, 512], BF16)]
    psc_sb = sb(attn, "psc_sb", [128, 19 * H], F32)
    causal_sb = sb(attn, "causal_sb", [128, 512], F32)
    selv_sb = sb(attn, "selv_sb", [128, 64 * NCH], BF16)
    selq_sb = sb(attn, "selq_sb", [128, 64 * NCH], BF16)
    selp_sb = sb(attn, "selp_sb", [16, 128 * NCH], F32)
    gnfin = sb(attn, "gnfin", [16, 1024], F32)
    gnsc = sb(attn, "gnsc", [16, 512], F32)
    for dst, src in ((psc_sb, c_pscale), (causal_sb, c_causal),
                     (selv_sb, c_selv), (selq_sb, c_selq),
                     (selp_sb, c_selp)):
        EM.op("sync", lambda e, d=dst, s=src: e.dma_start(out=d[:, :], in_=s[:, :]),
              waits=p12_bar, inc=dS)
    ev_tab = (dS.sem, dS.n)

    ev_wo, wbuf = w_dma("o", buf_free[wslot_i[0] % 2])
    SBANKS = (0, 1, 4, 5)
    ev_sco = {0: [ev_ps_free[0]], 1: [ev_ps_free[1]], 4: [], 5: []}
    ev_sco_sb = {0: None, 1: None, 2: None, 3: None}
    ev_pair = [ev_ps_free[2], ev_ps_free[3]]
    ev_g2 = [None, None]
    ev_xln2_blk = [None] * NQB
    ev_af_free = None
    ev_x2dma = None
    ev_attnN_free = None
    ev_xt3_free = ev_qk_done
    ev_xl2st = [None, None]

    qb_range = [0] if phase_limit == "p3s" else range(NQB)
    npr = int(os.environ.get("KPR", "8"))
    for qb in qb_range:
        ev_att_evac = {}
        ev_sco[4] = [st_["pb4"]]
        ev_sco[5] = [st_["pb5"]]
        for pr in range(NCH):
            pps = 2 + (pr % 2)
            ev_mm2_last = None
            for h2 in range(2):
                h = 2 * pr + h2
                p0 = h2 * 64
                nkt = 4 * qb + 4
                evacs_q = {}

                def emit_mm1(kt):
                    di = kt - 4 * qb
                    movw = 512 if di < 0 else 512 - 128 * di
                    sps = SBANKS[kt % 4]
                    col = 19 * h + (15 - 4 * qb + kt)
                    w1_ = [ev_qk_done] + ev_sco[sps]
                    if pr == 0 and h2 == 0 and kt == 0:
                        w1_ += [ev_tab, ev_pair[pps - 2]]
                    evm1 = EM.op("tensor", lambda e, h_=h, k_=kt, q_=qb,
                                 p0_=p0, s_=sps, mw=movw, pr_=pr:
                                 e.matmul(
                        pb[s_][0:128, 0:mw],
                        k_bf[p0_:p0_ + 64,
                             pr_ * T + 128 * k_:pr_ * T + 128 * (k_ + 1)],
                        q_bf[p0_:p0_ + 64,
                             pr_ * T + 512 * q_ + (128 * di if di >= 0 else 0):
                             pr_ * T + 512 * q_ + (128 * di if di >= 0 else 0)
                             + mw],
                        start=True, stop=True), waits=w1_, inc=cT)
                    wsc = [evm1, ev_sco_sb[kt % 4]]
                    if di < 0:
                        eve = EM.op("scalar", lambda e, s_=sps, c_=col, mw=movw,
                                    k_=kt: e.activation(
                            scob[k_ % 4][:, 0:mw], pb[s_][:, 0:mw], AF_.Copy,
                            scale=psc_sb[:, c_:c_ + 1]), waits=wsc, inc=cS)
                        evacs_q[kt] = [eve]
                    else:
                        evd = EM.op("vector", lambda e, s_=sps, c_=col, mw=movw,
                                    k_=kt: e.scalar_tensor_tensor(
                            scob[k_ % 4][:, 0:mw], pb[s_][:, 0:mw],
                            psc_sb[:, c_:c_ + 1], causal_sb[:, 0:mw],
                            ALU.mult, ALU.mult), waits=wsc, inc=cV)
                        evacs_q[kt] = [evd]
                    ev_sco[sps] = list(evacs_q[kt])

                def emit_mm2(kt):
                    di = kt - 4 * qb
                    movw = 512 if di < 0 else 512 - 128 * di
                    moff = 0 if di < 0 else 128 * di
                    wm2 = evacs_q[kt] + [ev_v_done]
                    if kt == 0 and h2 == 0:
                        wm2.append(ev_pair[pps - 2])
                    ev = EM.op("tensor", lambda e, k_=kt, h_=h, q_=qb,
                               p0_=p0, mw=movw, mo=moff, pp_=pps:
                               e.matmul(
                        pb[pp_][p0_:p0_ + 64, mo:mo + mw],
                        v_bf[:, k_ * 1024 + h_ * 64:k_ * 1024 + (h_ + 1) * 64],
                        scob[k_ % 4][:, 0:mw],
                        start=(k_ == 0), stop=(k_ == 4 * q_ + 3)),
                        waits=wm2, inc=cT)
                    ev_sco_sb[kt % 4] = ev
                    return ev

                LAG = 2
                for kt in range(nkt):
                    emit_mm1(kt)
                    if kt >= LAG:
                        ev_mm2_last = emit_mm2(kt - LAG)
                for kt in range(max(0, nkt - LAG), nkt):
                    ev_mm2_last = emit_mm2(kt)
            wg2 = [ev_g2[pr % 2]]
            evg = EM.op("sync", lambda e, p=pr: e.dma_start(
                out=g2_sb[p % 2][:, :], in_=c_G2[:, 512 * p:512 * (p + 1)]),
                waits=wg2, inc=dS)
            ev_att_evac[pr] = EM.op("vector", lambda e, p=pr: e.tensor_tensor(
                attnraw[:, 512 * p:512 * (p + 1)], pb[2 + (p % 2)][:, :],
                g2_sb[p % 2][:, :], ALU.mult),
                waits=[ev_mm2_last, evg], inc=cV)
            ev_g2[pr % 2] = ev_att_evac[pr]
            ev_pair[pr % 2] = ev_att_evac[pr]

        if "attnraw" in tap_out:
            for cc in range(NCH):
                EM.op("gpsimd", lambda e, c=cc, b=qb: e.dma_start(
                    out=tap_out["attnraw"][128 * c:128 * (c + 1),
                                           512 * b:512 * (b + 1)],
                    in_=attnraw[:, 512 * c:512 * (c + 1)]),
                    waits=allevs(), inc=dG)
        if phase_limit == "p3s":
            return finish(attn)

        ev_asq = [None, None]
        evgn_last = None
        for pr in range(NCH):
            sqb = asq[pr % 2]
            evsq = EM.op("scalar", lambda e, p=pr, d=sqb: e.activation(
                d[:, :], attnraw[:, 512 * p:512 * (p + 1)], AF_.Square),
                waits=[ev_att_evac[pr], ev_asq[pr % 2]], inc=cS)
            wmm = [ev_att_evac[pr]]
            if pr == 0:
                wmm.append(st_["pb6"])
            EM.op("tensor", lambda e, p=pr: e.matmul(
                pb[6][0:64, :], selv_sb[:, 64 * p:64 * (p + 1)],
                attnraw[:, 512 * p:512 * (p + 1)],
                start=(p == 0), stop=False), waits=wmm)
            ev_asq[pr % 2] = EM.op("tensor", lambda e, p=pr, d=sqb: e.matmul(
                pb[6][0:64, :], selq_sb[:, 64 * p:64 * (p + 1)],
                d[:, :], start=False, stop=(p == NCH - 1)),
                waits=[evsq], inc=cT)
            evgn_last = ev_asq[pr % 2]
        ev1 = EM.op("vector", lambda e: e.tensor_scalar(
            gnfin[:, 0:512], pb[6][0:16, :], 1.0 / DH, None, ALU.mult),
            waits=[evgn_last], inc=cV)
        ev2 = EM.op("vector", lambda e: e.tensor_scalar(
            gnfin[:, 512:1024], pb[6][32:48, :], 1.0 / DH, EPS,
            ALU.mult, ALU.add), waits=[evgn_last], inc=cV)
        st_["pb6"] = ev2
        ev3 = EM.op("vector", lambda e: e.scalar_tensor_tensor(
            gnsc[:, :], gnfin[:, 0:512], -1.0, gnfin[:, 0:512],
            ALU.mult, ALU.mult), waits=[ev1], inc=cV)
        ev4 = EM.op("vector", lambda e: e.tensor_tensor(
            gnfin[:, 512:1024], gnfin[:, 512:1024], gnsc[:, :], ALU.add),
            waits=[ev2, ev3], inc=cV)
        ev5 = EM.op("vector", lambda e: e.reciprocal(
            gnsc[:, :], gnfin[:, 512:1024]), waits=[ev4], inc=cV)
        ev6 = EM.op("scalar", lambda e: e.activation(
            gnfin[:, 512:1024], gnsc[:, :], AF_.Sqrt), waits=[ev5], inc=cS)
        ev7 = EM.op("vector", lambda e: e.scalar_tensor_tensor(
            gnfin[:, 0:512], gnfin[:, 0:512], -1.0, gnfin[:, 512:1024],
            ALU.mult, ALU.mult), waits=[ev6], inc=cV)
        ev_prev_nrm = None
        ev_nrm_last = None
        for pr in range(NCH):
            wbc = [ev7, ev_prev_nrm]
            if pr == 0:
                wbc += ev_sco[4] + ev_sco[5] + [st_["pb4"], st_["pb5"]]
            EM.op("tensor", lambda e, p=pr: e.matmul(
                pb[4][:, :], selp_sb[:, 128 * p:128 * (p + 1)],
                gnfin[:, 512:1024], start=True, stop=True), waits=wbc)
            evbg = EM.op("tensor", lambda e, p=pr: e.matmul(
                pb[5][:, :], selp_sb[:, 128 * p:128 * (p + 1)],
                gnfin[:, 0:512], start=True, stop=True), inc=cT)
            EM.op("vector", lambda e, p=pr: e.tensor_tensor(
                ttmp[:, :], attnraw[:, 512 * p:512 * (p + 1)], pb[4][:, :],
                ALU.mult), waits=[evbg])
            evt2 = EM.op("vector", lambda e: e.tensor_tensor(
                ttmp[:, :], ttmp[:, :], pb[5][:, :], ALU.add), inc=cV)
            ev_prev_nrm = evt2
            ev_nrm_last = EM.op("vector", lambda e, p=pr: e.tensor_scalar(
                attnN[:, 512 * p:512 * (p + 1)], ttmp[:, :],
                gnw_sb[:, p:p + 1], gnb_sb[:, p:p + 1], ALU.mult, ALU.add),
                waits=[evt2, ev_attnN_free], inc=cV)
        st_["pb4"] = ev_prev_nrm
        st_["pb5"] = ev_prev_nrm

        if "attnN" in tap_out:
            for cc in range(NCH):
                EM.op("gpsimd", lambda e, c=cc, b=qb: e.dma_start(
                    out=tap_out["attnN"][128 * c:128 * (c + 1),
                                         512 * b:512 * (b + 1)],
                    in_=attnN[:, 512 * c:512 * (c + 1)]),
                    waits=allevs(), inc=dG)

        ev_af = {}
        ev_wo_last = None
        for oc in range(NCH):
            par = oc % 2
            w = [ev_wo, ev_nrm_last] + ev_sco[par]
            evmm = None
            for cc in range(NCH):
                evmm = EM.op("tensor", lambda e, c=cc, o=oc, p=par,
                             wb=wbuf: e.matmul(
                    pb[p][:, :],
                    wb[:, c * 1024 + o * 128:c * 1024 + o * 128 + 128],
                    attnN[:, 512 * c:512 * (c + 1)],
                    start=(c == 0), stop=(c == NCH - 1)),
                    waits=(w if cc == 0 else []),
                    inc=(cT if cc == NCH - 1 else None))
            ev_wo_last = evmm
            wev = [evmm]
            if oc <= 1:
                wev += [ev_af_free, ev_x2dma]
            ev_af[oc] = EM.op("scalar", lambda e, o=oc, p=par: e.copy(
                af[:, 512 * o:512 * (o + 1)], pb[p][:, :]), waits=wev, inc=cS)
            ev_sco[par] = [ev_af[oc]]
        ev_attnN_free = ev_wo_last

        if "af" in tap_out:
            for cc in range(NCH):
                EM.op("gpsimd", lambda e, c=cc, b=qb: e.dma_start(
                    out=tap_out["af"][128 * c:128 * (c + 1),
                                      512 * b:512 * (b + 1)],
                    in_=af[:, 512 * c:512 * (c + 1)]), waits=allevs(), inc=dG)

        evb2 = ln_block(lambda cc: af[:, 512 * cc:512 * (cc + 1)],
                        lambda cc: ev_af[cc], xsq, ev_xsqf)
        ev_xr = {}
        for cc in range(NCH):
            ev_xr[cc] = EM.op("sync", lambda e, c=cc, b=qb: e.dma_start(
                out=xt2[0][:, 512 * c:512 * (c + 1)],
                in_=xfm[128 * c:128 * (c + 1), 512 * b:512 * (b + 1)]),
                waits=[ev_xt3_free], inc=dS)
        ev_x2 = {}
        evtt = None
        for cc in range(NCH):
            EM.op("vector", lambda e, c=cc: e.tensor_tensor(
                ttmp[:, :], af[:, 512 * c:512 * (c + 1)], pb[4][:, :],
                ALU.mult), waits=[evb2])
            EM.op("vector", lambda e: e.tensor_tensor(
                ttmp[:, :], ttmp[:, :], pb[5][:, :], ALU.add))
            evtt = EM.op("vector", lambda e, c=cc: e.tensor_tensor(
                af[:, 512 * c:512 * (c + 1)], ttmp[:, :],
                xt2[0][:, 512 * c:512 * (c + 1)], ALU.add),
                waits=[ev_xr[cc]], inc=cV)
            ev_x2[cc] = evtt
            ev_x2dma = EM.op("sync", lambda e, c=cc, b=qb: e.dma_start(
                out=x2_hbm[128 * c:128 * (c + 1), 512 * b:512 * (b + 1)],
                in_=af[:, 512 * c:512 * (c + 1)]), waits=[evtt], inc=dS)
        ev_xt3_free = evtt
        st_["pb4"] = evtt
        st_["pb5"] = evtt

        evb3 = ln_block(lambda cc: af[:, 512 * cc:512 * (cc + 1)],
                        lambda cc: ev_x2[cc], xsq, ev_xsqf)
        evtt = None
        for cc in range(NCH):
            par = cc % 2
            EM.op("vector", lambda e, c=cc: e.tensor_tensor(
                ttmp[:, :], af[:, 512 * c:512 * (c + 1)], pb[4][:, :],
                ALU.mult), waits=[evb3])
            evtt = EM.op("vector", lambda e, c=cc, p=par: e.tensor_tensor(
                xl2st[p][:, :], ttmp[:, :], pb[5][:, :], ALU.add),
                waits=[ev_xl2st[par]], inc=cV)
            ev_xl2st[par] = EM.op("sync", lambda e, c=cc, b=qb, p=par:
                                  e.dma_start(
                out=xln2_hbm[128 * c:128 * (c + 1), 512 * b:512 * (b + 1)],
                in_=xl2st[p][:, :]), waits=[evtt], inc=dS)
        ev_af_free = evtt
        st_["pb4"] = evtt
        st_["pb5"] = evtt
        ev_xln2_blk[qb] = ev_xl2st[1 - (NCH - 1) % 2] if False else evtt

    if "x2" in tap_out:
        EM.op("gpsimd", lambda e: e.dma_start(
            out=tap_out["x2"][:, :], in_=x2_hbm[:, :]), waits=allevs(), inc=dG)

    if phase_limit == "p3":
        return finish(attn)

    barrier_evs = allevs()
    attn.close()

    ffn = ExitStack()
    racc = sb(ffn, "racc", [128, NCH * T], F32)
    xln2_bf = sb(ffn, "xln2_bf", [128, NCH * T], BF16)
    rw_sb = sb(ffn, "rw_sb", [128, NCH * E], BF16)
    rb_sb = sb(ffn, "rb_sb", [1, E], F32)
    b1_sb = sb(ffn, "b1_sb", [128, 2 * E], F32)
    sb1_sb = sb(ffn, "sb1_sb", [128, 32], F32)
    b2_sb = sb(ffn, "b2_sb", [64, C], BF16)
    sb2_sb = sb(ffn, "sb2_sb", [2, C], F32)
    sb2f_sb = sb(ffn, "sb2f_sb", [128, 16], F32)
    exps_sb = sb(ffn, "exps_sb", [128, 2 * E], F32)
    t8_sb = sb(ffn, "t8_sb", [128, 8], F32)
    cwun_sb = sb(ffn, "cwun_sb", [128, 16 * E], BF16)
    nmax_sb = sb(ffn, "nmax_sb", [128, 16], F32)
    dsum_sb = sb(ffn, "dsum_sb", [128, 16], F32)
    rden_sb = sb(ffn, "rden_sb", [128, 16], F32)
    cwf_sb = sb(ffn, "cwf_sb", [128, 2 * E], F32)
    id64_sb = sb(ffn, "id64_sb", [64, 64], BF16)
    cw_em_bf = sb(ffn, "cw_em_bf", [64, T], BF16)
    moe = ExitStack()
    slot_cols = VPS * 4096
    wslot = sb(moe, "wslot", [128, NSLOT * slot_cols], BF16)
    g_all = sb(moe, "g_all", [128, 2 * VPS * 1024], BF16)

    ev_xln2_ld = None
    for cc in range(NCH):
        ev_xln2_ld = EM.op("sync", lambda e, c=cc: e.dma_start(
            out=xln2_bf[:, c * T:(c + 1) * T],
            in_=xln2_hbm[128 * c:128 * (c + 1), :]),
            waits=(barrier_evs if cc == 0 else []), inc=dS)
    if "xln2" in tap_out:
        EM.op("gpsimd", lambda e: e.dma_start(
            out=tap_out["xln2"][:, :], in_=xln2_hbm[:, :]),
            waits=allevs(), inc=dG)

    EM.op("gpsimd", lambda e: e.dma_start(
        out=rw_sb[:, :].rearrange("p (c e) -> p c e", c=NCH),
        in_=rw[:, :].rearrange("(c p) e -> p c e", p=128)),
        waits=barrier_evs, inc=dG)
    for dst, src in ((rb_sb, rb), (b1_sb, b1L), (sb1_sb, sb1L),
                     (sb2_sb, sb2d)):
        EM.op("sync", lambda e, d=dst, s=src: e.dma_start(
            out=d[:, :], in_=s[:, :]), waits=barrier_evs, inc=dS)
    for dst, src in ((b2_sb, b2d), (id64_sb, c_id64)):
        EM.op("gpsimd", lambda e, d=dst, s=src: e.dma_start(
            out=d[:, :], in_=s[:, :]), waits=barrier_evs, inc=dG)
    ev_ffn_pre = [dS, dG]

    for dc in range(NCH):
        EM.op("tensor", lambda e, d=dc: e.matmul(
            pb[6][:, 2 * d:2 * d + 2], sb2_sb[0:2, 128 * d:128 * (d + 1)],
            ones22_sb[:, :], start=(d == 0), stop=(d == NCH - 1)),
            waits=(ev_ffn_pre + [st_["pb6"]] if dc == 0 else []),
            inc=(cT if dc == NCH - 1 else None))
    ev_sb2mm = (cT.sem, cT.n)
    ev_sb2f = EM.op("scalar", lambda e: e.copy(
        sb2f_sb[:, :], pb[6][:, 0:16]), waits=[ev_sb2mm], inc=cS)
    st_["pb6"] = ev_sb2f

    ev_cwf_free = [None, None]
    for i in range(16):
        w = [ev_xln2_ld, st_["pb6"]] + ev_ffn_pre
        for cc in range(NCH):
            EM.op("tensor", lambda e, c=cc, ii=i: e.matmul(
                pb[6][:, 0:64],
                xln2_bf[:, c * T + 128 * ii:c * T + 128 * (ii + 1)],
                rw_sb[:, c * E:(c + 1) * E],
                start=(c == 0), stop=False), waits=(w if cc == 0 else []))
        evl = EM.op("tensor", lambda e: e.matmul(
            pb[6][:, 0:64], ones1_sb[:, :], rb_sb[:, :],
            start=False, stop=True), inc=cT)
        evnm = EM.op("vector", lambda e, ii=i: e.reduce_max(
            nmax_sb[:, ii:ii + 1], pb[6][:, 0:64], axis=AX.X, negate=True),
            waits=[evl], inc=cV)
        evex = EM.op("scalar", lambda e, ii=i: e.activation(
            exps_sb[:, (ii % 2) * E:(ii % 2 + 1) * E], pb[6][:, 0:64],
            AF_.Exp, bias=nmax_sb[:, ii:ii + 1]), waits=[evnm], inc=cS)
        st_["pb6"] = evex
        evt8 = EM.op("vector", lambda e, ii=i: e.max(
            out=t8_sb[:, :], in_=exps_sb[:, (ii % 2) * E:(ii % 2 + 1) * E]),
            waits=[evex], inc=cV)
        evcw = EM.op("vector", lambda e, ii=i: e.scalar_tensor_tensor(
            cwun_sb[:, ii * E:(ii + 1) * E],
            exps_sb[:, (ii % 2) * E:(ii % 2 + 1) * E], t8_sb[:, 7:8],
            exps_sb[:, (ii % 2) * E:(ii % 2 + 1) * E],
            ALU.is_ge, ALU.mult), waits=[evt8], inc=cV)
        ev_ds = EM.op("vector", lambda e, ii=i: e.reduce_sum(
            dsum_sb[:, ii:ii + 1], cwun_sb[:, ii * E:(ii + 1) * E], axis=AX.X),
            waits=[evcw], inc=cV)
    evrd = EM.op("vector", lambda e: e.reciprocal(rden_sb[:, :], dsum_sb[:, :]),
                 waits=[ev_ds], inc=cV)
    ev_cwem = None
    for i in range(16):
        evcf = EM.op("vector", lambda e, ii=i: e.tensor_scalar(
            cwf_sb[:, (ii % 2) * E:(ii % 2 + 1) * E],
            cwun_sb[:, ii * E:(ii + 1) * E], rden_sb[:, ii:ii + 1], None,
            ALU.mult), waits=[evrd, ev_cwf_free[i % 2]], inc=cV)
        evtr = EM.op("tensor", lambda e, ii=i: e.transpose(
            pb[7][0:64, 0:128], cwf_sb[:, (ii % 2) * E:(ii % 2 + 1) * E],
            ident_sb[:, :]), waits=[evcf, st_["pb7"]], inc=cT)
        ev_cwf_free[i % 2] = evtr
        ev_cwem = EM.op("scalar", lambda e, ii=i: e.copy(
            cw_em_bf[:, 128 * ii:128 * (ii + 1)], pb[7][0:64, 0:128]),
            waits=[evtr], inc=cS)
        st_["pb7"] = ev_cwem
    if "cwem" in tap_out:
        EM.op("gpsimd", lambda e: e.dma_start(
            out=tap_out["cwem"][:, :], in_=cw_em_bf[0:64, :]),
            waits=allevs(), inc=dG)
    if "cwun" in tap_out:
        EM.op("gpsimd", lambda e: e.dma_start(
            out=tap_out["cwun"][:, :], in_=cwun_sb[:, :]),
            waits=allevs(), inc=dG)
    if "rdbg" in tap_out:
        for off, buf in ((0, nmax_sb), (16, dsum_sb), (32, rden_sb)):
            EM.op("gpsimd", lambda e, o=off, b=buf: e.dma_start(
                out=tap_out["rdbg"][:, o:o + 16], in_=b[:, :]),
                waits=allevs(), inc=dG)

    if phase_limit == "p4":
        return finish(moe, ffn)

    vx = []
    for s in range(ES):
        for half in range(2):
            for j4 in range(4):
                vx.append(("sh", s, half, j4))
    for e_ in range(E):
        vx.append(("rt", e_))
    ngroups = (len(vx) + VPS - 1) // VPS
    groups = [vx[g * VPS:(g + 1) * VPS] for g in range(ngroups)]
    nrounds = (ngroups + 1) // 2
    last_routed_round = nrounds - 1

    def ve_dma(ve, slot, vi):
        base = slot * slot_cols + vi * 4096
        if ve[0] == "rt":
            e_ = ve[1]
            f1 = lambda e: e.dma_start(
                out=wslot[:, base:base + 2048].rearrange(
                    "p (c h) -> p c h", c=NCH),
                in_=w1[e_].rearrange("(c p) h -> p c h", p=128))
            f2 = lambda e: e.dma_start(
                out=wslot[:, base + 2048:base + 4096].rearrange(
                    "p (c d) -> p c d", c=2),
                in_=w2[e_].rearrange("(c p) d -> p c d", p=128))
        else:
            _, s, half, j4 = ve
            f1 = lambda e: e.dma_start(
                out=wslot[:, base:base + 2048].rearrange(
                    "p (c h) -> p c h", c=NCH),
                in_=sw1h[s, half, :, j4 * 256:(j4 + 1) * 256].rearrange(
                    "(c p) h -> p c h", p=128))
            f2 = lambda e: e.dma_start(
                out=wslot[:, base + 2048:base + 4096].rearrange(
                    "p (c d) -> p c d", c=2),
                in_=sw2[s][half * 1024 + j4 * 256:half * 1024 + (j4 + 1) * 256,
                           :].rearrange("(c p) d -> p c d", p=128))
        return f1, f2

    ev_slot_dma = {}
    ev_slot_freed = {}

    def emit_group_dma(g):
        slot = g % NSLOT
        w = list(barrier_evs)
        if g >= NSLOT:
            w.append(ev_slot_freed[g - NSLOT])
        for vi, ve in enumerate(groups[g]):
            f1, f2 = ve_dma(ve, slot, vi)
            EM.op("gpsimd", f1, waits=(w if vi == 0 else []), inc=dWs[slot])
            EM.op("gpsimd", f2, inc=dWs[slot])
        ev_slot_dma[g] = dWs[slot]

    for g in range(min(NSLOT, ngroups)):
        emit_group_dma(g)

    def b1ap(ve, hfc):
        if ve[0] == "rt":
            cc = ve[1] * 2 + hfc
            return b1_sb[:, cc:cc + 1]
        _, s, half, j4 = ve
        cc = s * 16 + half * 8 + j4 * 2 + hfc
        return sb1_sb[:, cc:cc + 1]

    ev_router_done = [(cV.sem, cV.n), (cS.sem, cS.n), (cT.sem, cT.n)]
    ev_hps = [list(barrier_evs) + [st_["pb6"]], list(barrier_evs)]
    ev_cwps = [list(barrier_evs), list(barrier_evs)]
    ev_flush = [ev_router_done, ev_router_done,
                ev_router_done + [st_["pb6"]], ev_router_done + [st_["pb7"]]]
    ev_g_free = [None] * (2 * VPS)
    ev_mm2 = None
    nrnd = (min(nrounds, int(os.environ.get("KRND", "99")))
            if phase_limit == "p5" else nrounds)
    sk5 = (os.environ.get("KSKIP5", "").split(",")
           if phase_limit == "p5" else [])
    for ri in range(nrnd):
        ves = []
        for g in (2 * ri, 2 * ri + 1):
            if g < ngroups:
                for vi, ve in enumerate(groups[g]):
                    ves.append((g, vi, ve))
        is_last = (ri == last_routed_round)
        for tc in range(NQB):
            ev_gready = {}
            for xi, (g, vi, ve) in enumerate(ves):
                par = xi % 2
                slot = g % NSLOT
                base = slot * slot_cols + vi * 4096
                routed = ve[0] == "rt"
                evcwb = None
                if routed:
                    evcwb = EM.op("tensor", lambda e, e_=ve[1], t_=tc, p=par:
                                  e.matmul(
                        pb[2 + p][:, :],
                        bcast_col(id64_sb[:, e_:e_ + 1], 128),
                        cw_em_bf[0:64, 512 * t_:512 * (t_ + 1)],
                        start=True, stop=True),
                        waits=[ev_cwem] + ev_cwps[par], inc=cT)
                for hfc in range(2):
                    w = [ev_slot_dma[g], ev_xln2_ld] + ev_hps[par]
                    if "mm1" in sk5:
                        ev = EM.op("tensor", lambda e: e.nop(), waits=w, inc=cT)
                        ev_hps[par] = [ev]
                        ev_gready[(xi, hfc)] = ev
                        continue
                    evh = None
                    for cc in range(NCH):
                        evh = EM.op("tensor", lambda e, c=cc, hf=hfc, b_=base,
                                    t_=tc, p=par: e.matmul(
                            pb[p][:, :],
                            wslot[:, b_ + c * 256 + hf * 128:
                                  b_ + c * 256 + hf * 128 + 128],
                            xln2_bf[:, c * T + 512 * t_:c * T + 512 * (t_ + 1)],
                            start=(c == 0), stop=(c == NCH - 1)),
                            waits=(w if cc == 0 else []),
                            inc=(cT if cc == NCH - 1 else None))
                    gslot = xi
                    wsl = [evh]
                    if hfc == 0:
                        wsl.append(ev_g_free[gslot])
                    if routed:
                        evs_ = EM.op("scalar", lambda e, hf=hfc, gs=gslot,
                                     p=par, v=ve: e.activation(
                            g_all[:, gs * 1024 + 512 * hf:
                                  gs * 1024 + 512 * (hf + 1)],
                            pb[p][:, :],
                            AF_.Silu, bias=b1ap(v, hf)), waits=wsl, inc=cS)
                        ev_hps[par] = [evs_]
                        evg = EM.op("vector", lambda e, hf=hfc, gs=gslot, p=par:
                                    e.tensor_tensor(
                            g_all[:, gs * 1024 + 512 * hf:
                                  gs * 1024 + 512 * (hf + 1)],
                            g_all[:, gs * 1024 + 512 * hf:
                                  gs * 1024 + 512 * (hf + 1)],
                            pb[2 + p][:, :], ALU.mult),
                            waits=[evs_, evcwb], inc=cV)
                        ev_cwps[par] = [evg]
                        ev_gready[(xi, hfc)] = evg
                    else:
                        evs_ = EM.op("scalar", lambda e, hf=hfc, gs=gslot,
                                     p=par, v=ve: e.activation(
                            g_all[:, gs * 1024 + 512 * hf:
                                  gs * 1024 + 512 * (hf + 1)],
                            pb[p][:, :], AF_.Silu, bias=b1ap(v, hf)),
                            waits=wsl, inc=cS)
                        ev_hps[par] = [evs_]
                        ev_gready[(xi, hfc)] = evs_
            for half in range(2):
                nves = len(ves)
                if "mm2" in sk5:
                    ev_mm2 = EM.op("tensor", lambda e: e.nop(),
                                   waits=[ev_gready[(xi2, 1)]
                                          for xi2 in range(nves)], inc=cT)
                    continue
                for xi, (g, vi, ve) in enumerate(ves):
                    slot = g % NSLOT
                    base = slot * slot_cols + vi * 4096
                    final = (xi == nves - 1)
                    for hfc in range(2):
                        for dc in range(NCH):
                            stt = (xi == 0 and hfc == 0 and dc % 2 == 0)
                            spp = (not is_last and final
                                   and hfc == 1 and dc % 2 == 1)
                            w = [ev_gready[(xi, hfc)]]
                            if stt:
                                w.append(ev_flush[dc // 2])
                            lastmm = (not is_last and final and hfc == 1
                                      and dc == NCH - 1)
                            ev = EM.op("tensor", lambda e, hf=hfc, d=dc,
                                       b_=base, gs=xi, h_=half, st=stt, sp=spp:
                                       e.matmul(
                                pb[4 + d // 2][:, (d % 2) * 256:
                                               (d % 2) * 256 + 256],
                                wslot[:, b_ + 2048 + hf * 1024 + d * 128:
                                      b_ + 2048 + hf * 1024 + d * 128 + 128],
                                g_all[:, gs * 1024 + hf * 512 + h_ * 256:
                                      gs * 1024 + hf * 512 + h_ * 256 + 256],
                                start=st, stop=sp), waits=w,
                                inc=(cT if lastmm else None))
                            if lastmm:
                                ev_mm2 = ev
                if is_last:
                    for dc in range(NCH):
                        ev_mm2 = EM.op("tensor", lambda e, d=dc, t_=tc, h_=half:
                                       e.matmul(
                            pb[4 + d // 2][:, (d % 2) * 256:(d % 2) * 256 + 256],
                            b2_sb[0:64, d * 128:(d + 1) * 128],
                            cw_em_bf[0:64, 512 * t_ + 256 * h_:
                                     512 * t_ + 256 * h_ + 256],
                            start=False, stop=(d % 2 == 1)), waits=[ev_cwem],
                            inc=(cT if dc == NCH - 1 else None))
                for jjx in range(4):
                    off = 512 * tc + 256 * half
                    outap = racc[:, :].rearrange(
                        "p (c t) -> p c t", c=NCH)[:, 2 * jjx:2 * jjx + 2,
                                                   off:off + 256]
                    inap = pb[4 + jjx][:, :].rearrange("p (r t) -> p r t", r=2)
                    if "flush" in sk5:
                        fn_ = (lambda e: e.engine_nop())
                    elif ri == 0:
                        fn_ = (lambda e, oa=outap, ia=inap:
                               e.tensor_copy(oa, ia))
                    else:
                        fn_ = (lambda e, oa=outap, ia=inap:
                               e.tensor_tensor(oa, oa, ia, ALU.add))
                    ev_flush[jjx] = EM.op("vector", fn_, waits=[ev_mm2], inc=cV)
            for xi in range(len(ves)):
                ev_g_free[xi] = ev_mm2
        for g in (2 * ri, 2 * ri + 1):
            if g < ngroups:
                ev_slot_freed[g] = ev_mm2
                if g + NSLOT < ngroups:
                    emit_group_dma(g + NSLOT)

    if "racc" in tap_out:
        for cc in range(NCH):
            EM.op("gpsimd", lambda e, c=cc: e.dma_start(
                out=tap_out["racc"][128 * c:128 * (c + 1), :],
                in_=racc[:, c * T:(c + 1) * T]), waits=allevs(), inc=dG)

    if phase_limit == "p5":
        return finish(moe, ffn)

    moe_bar = allevs()
    moe.close()

    epi = ExitStack()
    up = sb(epi, "up", [128, NCH * 512], F32)
    x2t = [sb(epi, "x2t0", [128, 512], F32), sb(epi, "x2t1", [128, 512], F32)]
    outb = [sb(epi, "outb0", [128, 512], F32), sb(epi, "outb1", [128, 512], F32)]
    xsq_e = [sb(epi, "xsqe0", [128, 512], F32), sb(epi, "xsqe1", [128, 512], F32)]
    ttmp_e = sb(epi, "ttmpe", [128, 512], F32)
    ev_xsqf_e = [moe_bar, moe_bar]
    st_["pb4"] = ev_flush[0]
    st_["pb5"] = ev_flush[1]
    st_["pb6"] = ev_flush[2]
    st_["pb7"] = ev_flush[3]
    ev_x2t = [moe_bar, moe_bar]
    ev_outb = [None, None]
    for tc in range(NQB):
        ev_u = {}
        for cc in range(NCH):
            w = [ev_sb2f] + ev_flush
            if tc == 0 and cc == 0:
                w = w + moe_bar
            ev_u[cc] = EM.op("vector", lambda e, c=cc, t_=tc: e.tensor_scalar(
                up[:, 512 * c:512 * (c + 1)],
                racc[:, c * T + 512 * t_:c * T + 512 * (t_ + 1)],
                sb2f_sb[:, 2 * c:2 * c + 1], None, ALU.add),
                waits=w, inc=cV)
        evb = ln_block(lambda cc: up[:, 512 * cc:512 * (cc + 1)],
                       lambda cc: ev_u[cc], xsq_e, ev_xsqf_e)
        evtt = None
        for cc in range(NCH):
            par = cc % 2
            evx = EM.op("sync", lambda e, c=cc, t_=tc, p=par: e.dma_start(
                out=x2t[p][:, :],
                in_=x2_hbm[128 * c:128 * (c + 1), 512 * t_:512 * (t_ + 1)]),
                waits=[ev_x2t[par]], inc=dS)
            EM.op("vector", lambda e, c=cc: e.tensor_tensor(
                ttmp_e[:, :], up[:, 512 * c:512 * (c + 1)], pb[4][:, :],
                ALU.mult), waits=[evb])
            EM.op("vector", lambda e: e.tensor_tensor(
                ttmp_e[:, :], ttmp_e[:, :], pb[5][:, :], ALU.add))
            evtt = EM.op("vector", lambda e, p=par: e.tensor_tensor(
                outb[p][:, :], ttmp_e[:, :], x2t[p][:, :], ALU.add),
                waits=[evx, ev_outb[par]], inc=cV)
            ev_x2t[par] = evtt
            ev_outb[par] = EM.op("sync", lambda e, c=cc, t_=tc, p=par:
                                 e.dma_start(
                out=out_fm[128 * c:128 * (c + 1), 512 * t_:512 * (t_ + 1)],
                in_=outb[p][:, :]), waits=[evtt], inc=dS)
        st_["pb4"] = evtt
        st_["pb5"] = evtt

    return finish(epi, ffn)


def host_inputs(inputs):
    asf = lambda a: np.ascontiguousarray(np.asarray(a), dtype=np.float32)
    Wq = asf(inputs["L0_attn_Wq"]); Wk = asf(inputs["L0_attn_Wk"])
    Wv = asf(inputs["L0_attn_Wv"]); Wo = asf(inputs["L0_attn_Wo"])
    b1 = asf(inputs["L0_ffn_b1"]); sb1 = asf(inputs["L0_ffn_sb1"])
    sw1 = asf(inputs["L0_ffn_sw1"])
    return {
        "wqT": np.ascontiguousarray(Wq.T), "wkT": np.ascontiguousarray(Wk.T),
        "wvT": np.ascontiguousarray(Wv.T), "woT": np.ascontiguousarray(Wo.T),
        "gnw8": np.ascontiguousarray(
            asf(inputs["L0_attn_gn_weight"]).reshape(8, 128).T),
        "gnb8": np.ascontiguousarray(
            asf(inputs["L0_attn_gn_bias"]).reshape(8, 128).T),
        "rw": asf(inputs["L0_ffn_router_w"]),
        "rb": asf(inputs["L0_ffn_router_b"]).reshape(1, E),
        "w1": asf(inputs["L0_ffn_w1"]),
        "b1L": np.ascontiguousarray(
            b1.reshape(E, 2, 128).transpose(2, 0, 1).reshape(128, 2 * E)),
        "w2": asf(inputs["L0_ffn_w2"]),
        "b2d": asf(inputs["L0_ffn_b2"]),
        "sw1h": np.ascontiguousarray(
            sw1.reshape(ES, C, 2, 1024).transpose(0, 2, 1, 3)),
        "sb1L": np.ascontiguousarray(
            sb1.reshape(ES, 16, 128).transpose(2, 0, 1).reshape(128, 32)),
        "sw2": asf(inputs["L0_ffn_sw2"]),
        "sb2d": asf(inputs["L0_ffn_sb2"]),
    }


def kernel(**inputs):
    xs = np.ascontiguousarray(np.asarray(inputs["L0_attn_x"]), dtype=np.float32)
    shared = host_inputs(inputs)
    nc = build_program()
    in_maps = []
    for b in range(B):
        m = dict(shared)
        m["xfm"] = np.ascontiguousarray(xs[b].T)
        in_maps.append(m)
    res = run_bass_kernel_spmd(nc, in_maps, core_ids=list(range(B)))
    out = np.empty((B, T, C), dtype=np.float32)
    for b in range(B):
        out[b] = res.results[b]["out_fm"].T
    return out


# version 05
# speedup vs baseline: 1.0097x; 1.0097x over previous
import dataclasses
import os
import numpy as np
import ml_dtypes
from contextlib import ExitStack

import concourse.bass as bass
import concourse.mybir as mybir
from concourse.bass_utils import run_bass_kernel_spmd

F32 = mybir.dt.float32
BF16 = mybir.dt.bfloat16
AF_ = mybir.ActivationFunctionType
ALU = mybir.AluOpType
AX = mybir.AxisListType
BF_NP = ml_dtypes.bfloat16

B, T, C, H, DH = 8, 2048, 1024, 16, 64
NCH = 8
NQB = 4
QBW = 512
E, HF = 64, 256
ES = 2
EPS = 1e-5
VPS = 3
NSLOT = 3

ENGINES = ("sync", "gpsimd", "tensor", "vector", "scalar")


class Ctr:
    def __init__(self, sem, step):
        self.sem = sem
        self.n = 0
        self.step = step


class Emitter:
    def __init__(self):
        self.q = {e: [] for e in ENGINES}

    def op(self, eng, fn, waits=(), inc=None):
        w = []
        def add(x):
            if x is None:
                return
            if isinstance(x, list):
                for y in x:
                    add(y)
            elif isinstance(x, Ctr):
                w.append((x.sem, x.n))
            else:
                w.append(x)
        for x in waits:
            add(x)
        ev = None
        if inc is not None:
            inc.n += inc.step
            ev = inc if inc.step == 16 else (inc.sem, inc.n)
        self.q[eng].append((w, fn, inc))
        return ev

    def runner(self, name):
        ops = self.q[name]

        def run(eng):
            waited = {}
            for w, fn, inc in ops:
                for sem, val in w:
                    k = id(sem)
                    if waited.get(k, -1) < val:
                        eng.wait_ge(sem, val)
                        waited[k] = val
                inst = fn(eng)
                if inc is not None:
                    inst.then_inc(inc.sem, inc.step)
        return run


def _gammas():
    return 1.0 - 2.0 ** (-5.0 - np.arange(H, dtype=np.float64))


def _rope_tables():
    inv_freq = (1.0 / (10000.0 ** (np.arange(0, DH, 2, dtype=np.float32)
                                   / np.float32(DH)))).astype(np.float32)
    t = np.arange(T, dtype=np.float32)
    freqs = (t[:, None] * inv_freq[None, :]).astype(np.float32)
    emb = np.concatenate([freqs, freqs], axis=-1)
    cos2 = np.concatenate([np.cos(emb).T] * 2, axis=0).astype(np.float32)
    sin2 = np.concatenate([np.sin(emb).T] * 2, axis=0).astype(np.float32)
    return cos2, sin2


def build_program(taps=(), phase_limit="all"):
    nc = bass.Bass("TRN2", target_bir_lowering=False,
                   detect_race_conditions=False)
    root = ExitStack()

    def din(name, shape):
        return nc.dram_tensor(name, list(shape), F32, kind="ExternalInput").ap()

    xfm = din("xfm", (C, T))
    wT = {"q": din("wqT", (C, C)), "k": din("wkT", (C, C)),
          "v": din("wvT", (C, C)), "o": din("woT", (C, C))}
    gnw8 = din("gnw8", (128, NCH)); gnb8 = din("gnb8", (128, NCH))
    rw = din("rw", (C, E)); rb = din("rb", (1, E))
    w1 = din("w1", (E, C, HF)); b1L = din("b1L", (128, 2 * E))
    w2 = din("w2", (E, HF, C)); b2d = din("b2d", (E, C))
    sw1h = din("sw1h", (ES, 2, C, 1024)); sb1L = din("sb1L", (128, 32))
    sw2 = din("sw2", (ES, 2048, C)); sb2d = din("sb2d", (ES, C))

    out_fm = nc.dram_tensor("out_fm", [C, T], F32, kind="ExternalOutput").ap()
    x2_hbm = nc.dram_tensor("x2_hbm", [C, T], F32, kind="Internal").ap()
    xln2_hbm = nc.dram_tensor("xln2_hbm", [C, T], BF16, kind="Internal").ap()

    tap_out = {}
    tap_shapes = {"xln1": (C, T), "q": (C, T), "k": (C, T), "v": (T, C),
                  "attnraw": (C, T), "attnN": (C, T), "af": (C, T),
                  "x2": (C, T), "xln2": (C, T), "cwem": (E, T),
                  "racc": (C, T), "cwun": (128, 1024), "rdbg": (128, 48)}
    for tp in taps:
        tap_out[tp] = nc.dram_tensor("tap_" + tp, list(tap_shapes[tp]), F32,
                                     kind="ExternalOutput").ap()

    gam = _gammas()
    cos2, sin2 = _rope_tables()
    c_cos = nc.inline_tensor(cos2.astype(BF_NP), name="c_cos").ap()
    c_sin = nc.inline_tensor(sin2.astype(BF_NP), name="c_sin").ap()
    Rr = np.zeros((DH, DH), dtype=np.float32)
    for dd in range(32):
        Rr[dd, dd + 32] = -1.0
        Rr[dd + 32, dd] = 1.0
    R2 = np.zeros((128, 128), dtype=np.float32)
    R2[:64, :64] = Rr; R2[64:, 64:] = Rr
    c_R2T = nc.inline_tensor(np.ascontiguousarray(R2.T), name="c_R2T").ap()
    c_ident = nc.inline_tensor(np.eye(128, dtype=np.float32), name="c_ident").ap()
    c_ones1 = nc.inline_tensor(np.ones((1, 128), np.float32), name="c_ones1").ap()
    c_ones2c = nc.inline_tensor(np.ones((128, 2), np.float32), name="c_ones2c").ap()
    c_ones22 = nc.inline_tensor(np.ones((2, 2), np.float32), name="c_ones22").ap()
    cz = np.ones((128, 512), np.float32)
    for jk in range(128):
        cz[jk, 0:jk] = 0.0
    c_causal = nc.inline_tensor(cz, name="c_causal").ap()
    ps_all = np.zeros((128, 19 * H), np.float64)
    jj_ = np.arange(128, dtype=np.float64)
    for h in range(H):
        for ccx in range(19):
            ps_all[:, 19 * h + ccx] = gam[h] ** ((1920.0 - 128.0 * ccx) - jj_) / 8.0
    c_pscale = nc.inline_tensor(ps_all.astype(np.float32), name="c_pscale").ap()
    g2 = np.zeros((128, NCH * QBW), np.float64)
    jq_ = np.arange(QBW, dtype=np.float64)
    for pr in range(NCH):
        g2[:64, pr * QBW:(pr + 1) * QBW] = gam[2 * pr] ** jq_
        g2[64:, pr * QBW:(pr + 1) * QBW] = gam[2 * pr + 1] ** jq_
    c_G2 = nc.inline_tensor(g2.astype(np.float32).astype(BF_NP), name="c_G2").ap()
    selv = np.zeros((128, 64 * NCH), np.float32)
    selq = np.zeros((128, 64 * NCH), np.float32)
    for pr in range(NCH):
        selv[:64, 64 * pr + 2 * pr] = 1.0
        selv[64:, 64 * pr + 2 * pr + 1] = 1.0
        selq[:64, 64 * pr + 32 + 2 * pr] = 1.0
        selq[64:, 64 * pr + 32 + 2 * pr + 1] = 1.0
    c_selv = nc.inline_tensor(selv.astype(BF_NP), name="c_selv").ap()
    c_selq = nc.inline_tensor(selq.astype(BF_NP), name="c_selq").ap()
    selp = np.zeros((16, 128 * NCH), np.float32)
    for pr in range(NCH):
        selp[2 * pr, 128 * pr:128 * pr + 64] = 1.0
        selp[2 * pr + 1, 128 * pr + 64:128 * pr + 128] = 1.0
    c_selp = nc.inline_tensor(selp, name="c_selp").ap()
    rowsel = np.zeros((8, 8 * 128), np.float32)
    for r8 in range(8):
        rowsel[r8, 128 * r8:128 * (r8 + 1)] = 1.0
    c_rowsel = nc.inline_tensor(rowsel, name="c_rowsel").ap()
    c_id64 = nc.inline_tensor(np.eye(64, dtype=np.float32).astype(BF_NP),
                              name="c_id64").ap()

    def bcast_col(ap_col, n):
        return dataclasses.replace(
            ap_col, ap=[[ap_col.ap[0][0], ap_col.shape[0]], [0, n]])

    dS = Ctr(root.enter_context(nc.semaphore("dS")), 16)
    dG = Ctr(root.enter_context(nc.semaphore("dG")), 16)
    dW0 = Ctr(root.enter_context(nc.semaphore("dW0")), 16)
    dW1 = Ctr(root.enter_context(nc.semaphore("dW1")), 16)
    dW2 = Ctr(root.enter_context(nc.semaphore("dW2")), 16)
    dWs = [dW0, dW1, dW2]
    cT = Ctr(root.enter_context(nc.semaphore("cT")), 1)
    cV = Ctr(root.enter_context(nc.semaphore("cV")), 1)
    cS = Ctr(root.enter_context(nc.semaphore("cS")), 1)

    EM = Emitter()

    def finish(*stacks):
        EM.op("sync", lambda e: e.nop(),
              waits=[dS, dG, dW0, dW1, dW2])
        with nc.Block() as block:
            block.sync(EM.runner("sync"))
            block.gpsimd(EM.runner("gpsimd"))
            block.tensor(EM.runner("tensor"))
            block.vector(EM.runner("vector"))
            block.scalar(EM.runner("scalar"))
        for s_ in stacks:
            s_.close()
        root.close()
        return nc

    def allevs():
        return [dS, dG, dW0, dW1, dW2,
                (cT.sem, cT.n), (cV.sem, cV.n), (cS.sem, cS.n)]

    def sb(stack, name, shape, dtype):
        return stack.enter_context(nc.sbuf_tensor(name, list(shape), dtype))

    gnw_sb = sb(root, "gnw_sb", [128, NCH], F32)
    gnb_sb = sb(root, "gnb_sb", [128, NCH], F32)
    ones1_sb = sb(root, "ones1_sb", [1, 128], F32)
    ones2c_sb = sb(root, "ones2c_sb", [128, 2], F32)
    ones22_sb = sb(root, "ones22_sb", [2, 2], F32)
    ident_sb = sb(root, "ident_sb", [128, 128], F32)
    lnr_sb = sb(root, "lnr_sb", [1, 1536], F32)

    pb = [root.enter_context(nc.psum_tensor("pbk%d" % i, [128, 512], F32))
          for i in range(8)]

    for dst, src in ((gnw_sb, gnw8), (gnb_sb, gnb8), (ones1_sb, c_ones1),
                     (ones2c_sb, c_ones2c), (ones22_sb, c_ones22),
                     (ident_sb, c_ident)):
        EM.op("sync", lambda e, d=dst, s=src: e.dma_start(out=d[:, :], in_=s[:, :]),
              inc=dS)
    ev_glob = dS

    st_ = {"pb4": None, "pb5": None, "pb6": None, "pb7": None,
           "fin": None, "rows": None}

    def ln_block(src, ev_src, xsq_p, ev_xsqf):
        ev_sq = {}
        ev_mm = None
        for cc in range(NCH):
            sqb = xsq_p[cc % 2]
            w = [ev_src(cc), ev_xsqf[cc % 2]]
            ev_sq[cc] = EM.op("scalar",
                              lambda e, d=sqb, s=src(cc): e.activation(
                                  d[:, :], s, AF_.Square),
                              waits=w, inc=cS)
            first = (cc == 0)
            wv = [ev_src(cc), ev_glob]
            if first:
                wv.append(st_["pb6"])
            EM.op("tensor",
                  lambda e, s=src(cc), f=first, sp2=(cc == NCH - 1): e.matmul(
                      pb[6][0:2, :], ones2c_sb[:, :], s,
                      start=f, stop=sp2), waits=wv)
            sp = (cc == NCH - 1)
            wq = [ev_sq[cc]]
            if first:
                wq.append(st_["pb7"])
            ev = EM.op("tensor",
                       lambda e, d=sqb, f=first, sp=sp: e.matmul(
                           pb[7][0:2, :], ones2c_sb[:, :], d[:, :],
                           start=f, stop=sp),
                       waits=wq, inc=cT)
            ev_xsqf[cc % 2] = ev
            if sp:
                ev_mm = ev
        A, Bc, Cc = lnr_sb[:, 0:512], lnr_sb[:, 512:1024], lnr_sb[:, 1024:1536]
        ev1 = EM.op("vector", lambda e: e.tensor_scalar(
            A, pb[6][0:1, :], 1.0 / C, None, ALU.mult),
            waits=[ev_mm, st_["fin"]], inc=cV)
        ev2 = EM.op("vector", lambda e: e.tensor_scalar(
            Bc, pb[7][0:1, :], 1.0 / C, EPS, ALU.mult, ALU.add),
            waits=[ev_mm], inc=cV)
        st_["pb6"] = ev1
        st_["pb7"] = ev2
        ev3 = EM.op("vector", lambda e: e.scalar_tensor_tensor(
            Cc, A, -1.0, A, ALU.mult, ALU.mult), waits=[ev1], inc=cV)
        ev4 = EM.op("vector", lambda e: e.tensor_tensor(
            Bc, Bc, Cc, ALU.add), waits=[ev2, ev3], inc=cV)
        ev5 = EM.op("vector", lambda e: e.reciprocal(Cc, Bc),
                    waits=[ev4], inc=cV)
        ev6 = EM.op("scalar", lambda e: e.activation(Bc, Cc, AF_.Sqrt),
                    waits=[ev5], inc=cS)
        ev7 = EM.op("vector", lambda e: e.scalar_tensor_tensor(
            A, A, -1.0, Bc, ALU.mult, ALU.mult), waits=[ev6], inc=cV)
        st_["fin"] = ev7
        EM.op("tensor", lambda e: e.matmul(
            pb[4][:, :], ones1_sb[:, :], Bc, start=True, stop=True),
            waits=[ev6, ev7, st_["pb4"], st_["pb5"]])
        evb = EM.op("tensor", lambda e: e.matmul(
            pb[5][:, :], ones1_sb[:, :], A, start=True, stop=True),
            waits=[ev7], inc=cT)
        return evb

    attn = ExitStack()
    q_bf = sb(attn, "q_bf", [128, NCH * T], BF16)
    k_bf = sb(attn, "k_bf", [128, NCH * T], BF16)
    v_bf = sb(attn, "v_bf", [128, 16 * 1024], BF16)
    xt_one = sb(attn, "xtA", [128, NCH * 512], F32)
    xt2 = [xt_one, xt_one]
    xsq = [sb(attn, "xsq0", [128, 512], F32), sb(attn, "xsq1", [128, 512], F32)]
    ttmp = sb(attn, "ttmp", [128, 512], F32)
    wbuf2 = [sb(attn, "wbufA", [128, NCH * 1024], BF16),
             sb(attn, "wbufB", [128, NCH * 1024], BF16)]
    xl2st = [sb(attn, "xl2st0", [128, 512], BF16),
             sb(attn, "xl2st1", [128, 512], BF16)]
    p12 = ExitStack()
    xln1_bf = sb(p12, "xln1_bf", [128, NCH * T], BF16)
    cos_sb = sb(p12, "cos_sb", [128, T], BF16)
    sin_sb = sb(p12, "sin_sb", [128, T], BF16)
    r2t_sb = sb(p12, "r2t_sb", [128, 128], F32)

    for dst, src in ((cos_sb, c_cos), (sin_sb, c_sin), (r2t_sb, c_R2T)):
        EM.op("sync", lambda e, d=dst, s=src: e.dma_start(out=d[:, :], in_=s[:, :]),
              inc=dS)
    ev_tab = dS

    ev_xsqf = [None, None]
    ev_xt = {}
    ev_xt_free = [None, None]
    ev_xln1_blk = [None] * NQB
    for blk in range(NQB):
        xt = xt2[blk % 2]
        for cc in range(NCH):
            ev_xt[cc] = EM.op("sync", lambda e, c=cc, b=blk, x_=xt:
                              e.dma_start(
                out=x_[:, 512 * c:512 * (c + 1)],
                in_=xfm[128 * c:128 * (c + 1), 512 * b:512 * (b + 1)]),
                waits=[ev_xt_free[0], ev_xt_free[1]], inc=dS)
        evb = ln_block(lambda cc: xt[:, 512 * cc:512 * (cc + 1)],
                       lambda cc: ev_xt[cc], xsq, ev_xsqf)
        evtt = None
        for cc in range(NCH):
            EM.op("vector", lambda e, c=cc, x_=xt: e.tensor_tensor(
                ttmp[:, :], x_[:, 512 * c:512 * (c + 1)], pb[4][:, :],
                ALU.mult), waits=[evb])
            evtt = EM.op("vector", lambda e, c=cc, b=blk: e.tensor_tensor(
                xln1_bf[:, c * T + 512 * b:c * T + 512 * (b + 1)],
                ttmp[:, :], pb[5][:, :], ALU.add), inc=cV)
        ev_xt_free[blk % 2] = evtt
        st_["pb4"] = evtt
        st_["pb5"] = evtt
        ev_xln1_blk[blk] = evtt

    if "xln1" in tap_out:
        for cc in range(NCH):
            EM.op("gpsimd", lambda e, c=cc: e.dma_start(
                out=tap_out["xln1"][128 * c:128 * (c + 1), :],
                in_=xln1_bf[:, c * T:(c + 1) * T]), waits=allevs(), inc=dG)
    if phase_limit == "p1":
        return finish(p12, attn)

    wslot_i = [0]

    def w_dma(which, ev_free):
        buf = wbuf2[wslot_i[0] % 2]
        wslot_i[0] += 1
        last = None
        for piece in range(4):
            w = [ev_free] if piece == 0 else []
            last = EM.op("gpsimd", lambda e, p=piece, t=wT[which], b=buf:
                         e.dma_start(
                out=b[:, 2048 * p:2048 * (p + 1)].rearrange(
                    "p (a o) -> p a o", a=2),
                in_=t[256 * p:256 * (p + 1), :].rearrange(
                    "(a p) o -> p a o", p=128)), waits=w, inc=dG)
        return last, buf

    ev_wv, wbuf = w_dma("v", None)
    ev_vps = {0: None, 1: None, 2: None, 3: None}
    ev_v_done = None
    for i in range(16):
        blk = i // 4
        pp = (i % 2) * 2
        for oh in range(2):
            bank = pp + oh
            w = [ev_wv, ev_xln1_blk[blk], ev_vps[bank]]
            evmm = None
            for cc in range(NCH):
                evmm = EM.op("tensor", lambda e, c=cc, ii=i, o=oh, bk=bank,
                             wb=wbuf: e.matmul(
                    pb[bk][:, :],
                    xln1_bf[:, c * T + 128 * ii:c * T + 128 * (ii + 1)],
                    wb[:, c * 1024 + o * 512:c * 1024 + (o + 1) * 512],
                    start=(c == 0), stop=(c == NCH - 1)),
                    waits=(w if cc == 0 else []),
                    inc=(cT if cc == NCH - 1 else None))
            ev_vps[bank] = EM.op("scalar", lambda e, ii=i, o=oh, bk=bank: e.copy(
                v_bf[:, ii * 1024 + o * 512:ii * 1024 + (o + 1) * 512],
                pb[bk][:, :]), waits=[evmm], inc=cS)
            ev_v_done = ev_vps[bank]

    if "v" in tap_out:
        for i in range(16):
            EM.op("gpsimd", lambda e, ii=i: e.dma_start(
                out=tap_out["v"][128 * ii:128 * (ii + 1), :],
                in_=v_bf[:, ii * 1024:(ii + 1) * 1024]), waits=allevs(), inc=dG)

    qk_dst = {"q": q_bf, "k": k_bf}
    ev_ps_free = {0: ev_vps[0], 1: ev_vps[1], 2: ev_vps[2], 3: ev_vps[3]}
    ev_qtmp = [[ev_xt_free[0]], [ev_xt_free[0]]]
    ev_qk_done = None
    buf_free = {0: ev_v_done, 1: None}
    for which in ("q", "k"):
        ev_w, wbuf = w_dma(which, buf_free[wslot_i[0] % 2])
        dst = qk_dst[which]
        for oc in range(NCH):
            for blk in range(NQB):
                par = (oc * NQB + blk) % 2
                w = [ev_w, ev_xln1_blk[blk], ev_tab, ev_ps_free[par]]
                evmm = None
                for cc in range(NCH):
                    evmm = EM.op("tensor", lambda e, c=cc, o=oc, b=blk, p=par,
                                 wb=wbuf: e.matmul(
                        pb[p][:, :],
                        wb[:, c * 1024 + o * 128:c * 1024 + o * 128 + 128],
                        xln1_bf[:, c * T + 512 * b:c * T + 512 * (b + 1)],
                        start=(c == 0), stop=(c == NCH - 1)),
                        waits=(w if cc == 0 else []),
                        inc=(cT if cc == NCH - 1 else None))
                evq = EM.op("scalar", lambda e, p=par: e.copy(
                    xt2[0][:, 512 * p:512 * (p + 1)], pb[p][:, :]),
                    waits=[evmm] + ev_qtmp[par], inc=cS)
                ev_ps_free[par] = evq
                evr = EM.op("tensor", lambda e, p=par: e.matmul(
                    pb[2 + p][:, :], r2t_sb[:, :],
                    xt2[0][:, 512 * p:512 * (p + 1)],
                    start=True, stop=True),
                    waits=[evq, ev_ps_free[2 + par]], inc=cT)
                evt1 = EM.op("vector", lambda e, p=par, b=blk: e.tensor_tensor(
                    xt2[0][:, 1024 + 512 * p:1024 + 512 * (p + 1)],
                    xt2[0][:, 512 * p:512 * (p + 1)],
                    cos_sb[:, 512 * b:512 * (b + 1)], ALU.mult),
                    waits=[evq], inc=cV)
                evt2 = EM.op("vector", lambda e, p=par, b=blk: e.tensor_tensor(
                    xt2[0][:, 2048 + 512 * p:2048 + 512 * (p + 1)],
                    pb[2 + p][:, :], sin_sb[:, 512 * b:512 * (b + 1)],
                    ALU.mult), waits=[evr], inc=cV)
                ev_ps_free[2 + par] = evt2
                evo = EM.op("vector", lambda e, p=par, o=oc, b=blk, d=dst:
                            e.tensor_tensor(
                    d[:, o * T + 512 * b:o * T + 512 * (b + 1)],
                    xt2[0][:, 1024 + 512 * p:1024 + 512 * (p + 1)],
                    xt2[0][:, 2048 + 512 * p:2048 + 512 * (p + 1)], ALU.add),
                    inc=cV)
                ev_qtmp[par] = [evr, evt1, evo]
                ev_qk_done = evo
        buf_free[1 - wslot_i[0] % 2] = evmm

    for tp, buf in (("q", q_bf), ("k", k_bf)):
        if tp in tap_out:
            for cc in range(NCH):
                EM.op("gpsimd", lambda e, c=cc, b=buf, t_=tp: e.dma_start(
                    out=tap_out[t_][128 * c:128 * (c + 1), :],
                    in_=b[:, c * T:(c + 1) * T]), waits=allevs(), inc=dG)

    if phase_limit == "p2":
        return finish(p12, attn)

    p12_bar = allevs()
    p12.close()
    af = sb(attn, "af", [128, NCH * 512], F32)
    scob = [sb(attn, "scob%d" % i, [128, 512], BF16) for i in range(4)]
    attnraw = sb(attn, "attnraw", [128, NCH * 512], BF16)
    attnN = attnraw
    asq = [sb(attn, "asq0", [128, 512], BF16), sb(attn, "asq1", [128, 512], BF16)]
    g2_sb = [sb(attn, "g2a", [128, 512], BF16), sb(attn, "g2b", [128, 512], BF16)]
    psc_sb = sb(attn, "psc_sb", [128, 19 * H], F32)
    causal_sb = sb(attn, "causal_sb", [128, 512], F32)
    selv_sb = sb(attn, "selv_sb", [128, 64 * NCH], BF16)
    selq_sb = sb(attn, "selq_sb", [128, 64 * NCH], BF16)
    selp_sb = sb(attn, "selp_sb", [16, 128 * NCH], F32)
    gnfin = sb(attn, "gnfin", [16, 1024], F32)
    gnsc = sb(attn, "gnsc", [16, 512], F32)
    for dst, src in ((psc_sb, c_pscale), (causal_sb, c_causal),
                     (selv_sb, c_selv), (selq_sb, c_selq),
                     (selp_sb, c_selp)):
        EM.op("sync", lambda e, d=dst, s=src: e.dma_start(out=d[:, :], in_=s[:, :]),
              waits=p12_bar, inc=dS)
    ev_tab = (dS.sem, dS.n)

    ev_wo, wbuf = w_dma("o", buf_free[wslot_i[0] % 2])
    SBANKS = (0, 1)
    ev_sco = {0: [ev_ps_free[0]], 1: [ev_ps_free[1]]}
    ev_sco_sb = {0: None, 1: None, 2: None, 3: None}
    ev_pair = [ev_ps_free[2], ev_ps_free[3]]
    ev_g2 = [None, None]
    ev_xln2_blk = [None] * NQB
    ev_af_free = None
    ev_x2dma = None
    ev_attnN_free = None
    ev_xt3_free = ev_qk_done
    ev_xl2st = [None, None]

    def emit_ln_section(qb, ev_af):
        nonlocal ev_xt3_free, ev_af_free, ev_x2dma
        evb2 = ln_block(lambda cc: af[:, 512 * cc:512 * (cc + 1)],
                        lambda cc: ev_af[cc], xsq, ev_xsqf)
        ev_xr = {}
        for cc in range(NCH):
            ev_xr[cc] = EM.op("sync", lambda e, c=cc, b=qb: e.dma_start(
                out=xt2[0][:, 512 * c:512 * (c + 1)],
                in_=xfm[128 * c:128 * (c + 1), 512 * b:512 * (b + 1)]),
                waits=[ev_xt3_free], inc=dS)
        ev_x2 = {}
        evtt = None
        for cc in range(NCH):
            EM.op("vector", lambda e, c=cc: e.tensor_tensor(
                ttmp[:, :], af[:, 512 * c:512 * (c + 1)], pb[4][:, :],
                ALU.mult), waits=[evb2])
            EM.op("vector", lambda e: e.tensor_tensor(
                ttmp[:, :], ttmp[:, :], pb[5][:, :], ALU.add))
            evtt = EM.op("vector", lambda e, c=cc: e.tensor_tensor(
                af[:, 512 * c:512 * (c + 1)], ttmp[:, :],
                xt2[0][:, 512 * c:512 * (c + 1)], ALU.add),
                waits=[ev_xr[cc]], inc=cV)
            ev_x2[cc] = evtt
            ev_x2dma = EM.op("sync", lambda e, c=cc, b=qb: e.dma_start(
                out=x2_hbm[128 * c:128 * (c + 1), 512 * b:512 * (b + 1)],
                in_=af[:, 512 * c:512 * (c + 1)]), waits=[evtt], inc=dS)
        ev_xt3_free = evtt
        st_["pb4"] = evtt
        st_["pb5"] = evtt
        evb3 = ln_block(lambda cc: af[:, 512 * cc:512 * (cc + 1)],
                        lambda cc: ev_x2[cc], xsq, ev_xsqf)
        evtt = None
        for cc in range(NCH):
            par = cc % 2
            EM.op("vector", lambda e, c=cc: e.tensor_tensor(
                ttmp[:, :], af[:, 512 * c:512 * (c + 1)], pb[4][:, :],
                ALU.mult), waits=[evb3])
            evtt = EM.op("vector", lambda e, c=cc, p=par: e.tensor_tensor(
                xl2st[p][:, :], ttmp[:, :], pb[5][:, :], ALU.add),
                waits=[ev_xl2st[par]], inc=cV)
            ev_xl2st[par] = EM.op("sync", lambda e, c=cc, b=qb, p=par:
                                  e.dma_start(
                out=xln2_hbm[128 * c:128 * (c + 1), 512 * b:512 * (b + 1)],
                in_=xl2st[p][:, :]), waits=[evtt], inc=dS)
        ev_af_free = evtt
        st_["pb4"] = evtt
        st_["pb5"] = evtt
        ev_xln2_blk[qb] = evtt

    ln_sections = []
    qb_range = [0] if phase_limit == "p3s" else range(NQB)
    npr = int(os.environ.get("KPR", "8"))
    for qb in qb_range:
        ev_att_evac = {}
        for pr in range(NCH):
            pps = 2 + (pr % 2)
            ev_mm2_last = None
            for h2 in range(2):
                h = 2 * pr + h2
                p0 = h2 * 64
                nkt = 4 * qb + 4
                evacs_q = {}

                def emit_mm1(kt):
                    di = kt - 4 * qb
                    movw = 512 if di < 0 else 512 - 128 * di
                    sps = SBANKS[kt % 2]
                    col = 19 * h + (15 - 4 * qb + kt)
                    w1_ = [ev_qk_done] + ev_sco[sps]
                    if pr == 0 and h2 == 0 and kt == 0:
                        w1_ += [ev_tab, ev_pair[pps - 2]]
                    evm1 = EM.op("tensor", lambda e, h_=h, k_=kt, q_=qb,
                                 p0_=p0, s_=sps, mw=movw, pr_=pr:
                                 e.matmul(
                        pb[s_][0:128, 0:mw],
                        k_bf[p0_:p0_ + 64,
                             pr_ * T + 128 * k_:pr_ * T + 128 * (k_ + 1)],
                        q_bf[p0_:p0_ + 64,
                             pr_ * T + 512 * q_ + (128 * di if di >= 0 else 0):
                             pr_ * T + 512 * q_ + (128 * di if di >= 0 else 0)
                             + mw],
                        start=True, stop=True), waits=w1_, inc=cT)
                    wsc = [evm1, ev_sco_sb[kt % 4]]
                    if di < 0:
                        eve = EM.op("scalar", lambda e, s_=sps, c_=col, mw=movw,
                                    k_=kt: e.activation(
                            scob[k_ % 4][:, 0:mw], pb[s_][:, 0:mw], AF_.Copy,
                            scale=psc_sb[:, c_:c_ + 1]), waits=wsc, inc=cS)
                        evacs_q[kt] = [eve]
                    else:
                        evd = EM.op("vector", lambda e, s_=sps, c_=col, mw=movw,
                                    k_=kt: e.scalar_tensor_tensor(
                            scob[k_ % 4][:, 0:mw], pb[s_][:, 0:mw],
                            psc_sb[:, c_:c_ + 1], causal_sb[:, 0:mw],
                            ALU.mult, ALU.mult), waits=wsc, inc=cV)
                        evacs_q[kt] = [evd]
                    ev_sco[sps] = list(evacs_q[kt])

                def emit_mm2(kt):
                    di = kt - 4 * qb
                    movw = 512 if di < 0 else 512 - 128 * di
                    moff = 0 if di < 0 else 128 * di
                    wm2 = evacs_q[kt] + [ev_v_done]
                    if kt == 0 and h2 == 0:
                        wm2.append(ev_pair[pps - 2])
                    ev = EM.op("tensor", lambda e, k_=kt, h_=h, q_=qb,
                               p0_=p0, mw=movw, mo=moff, pp_=pps:
                               e.matmul(
                        pb[pp_][p0_:p0_ + 64, mo:mo + mw],
                        v_bf[:, k_ * 1024 + h_ * 64:k_ * 1024 + (h_ + 1) * 64],
                        scob[k_ % 4][:, 0:mw],
                        start=(k_ == 0), stop=(k_ == 4 * q_ + 3)),
                        waits=wm2, inc=cT)
                    ev_sco_sb[kt % 4] = ev
                    return ev

                LAG = 2
                for kt in range(nkt):
                    emit_mm1(kt)
                    if kt >= LAG:
                        ev_mm2_last = emit_mm2(kt - LAG)
                for kt in range(max(0, nkt - LAG), nkt):
                    ev_mm2_last = emit_mm2(kt)
            wg2 = [ev_g2[pr % 2]]
            evg = EM.op("sync", lambda e, p=pr: e.dma_start(
                out=g2_sb[p % 2][:, :], in_=c_G2[:, 512 * p:512 * (p + 1)]),
                waits=wg2, inc=dS)
            ev_att_evac[pr] = EM.op("vector", lambda e, p=pr: e.tensor_tensor(
                attnraw[:, 512 * p:512 * (p + 1)], pb[2 + (p % 2)][:, :],
                g2_sb[p % 2][:, :], ALU.mult),
                waits=[ev_mm2_last, evg], inc=cV)
            ev_g2[pr % 2] = ev_att_evac[pr]
            ev_pair[pr % 2] = ev_att_evac[pr]

        if "attnraw" in tap_out:
            for cc in range(NCH):
                EM.op("gpsimd", lambda e, c=cc, b=qb: e.dma_start(
                    out=tap_out["attnraw"][128 * c:128 * (c + 1),
                                           512 * b:512 * (b + 1)],
                    in_=attnraw[:, 512 * c:512 * (c + 1)]),
                    waits=allevs(), inc=dG)
        if phase_limit == "p3s":
            return finish(attn)
        if qb > 0:
            emit_ln_section(*ln_sections[qb - 1])

        ev_asq = [None, None]
        evgn_last = None
        for pr in range(NCH):
            sqb = asq[pr % 2]
            evsq = EM.op("scalar", lambda e, p=pr, d=sqb: e.activation(
                d[:, :], attnraw[:, 512 * p:512 * (p + 1)], AF_.Square),
                waits=[ev_att_evac[pr], ev_asq[pr % 2]], inc=cS)
            wmm = [ev_att_evac[pr]]
            if pr == 0:
                wmm.append(st_["pb6"])
            EM.op("tensor", lambda e, p=pr: e.matmul(
                pb[6][0:64, :], selv_sb[:, 64 * p:64 * (p + 1)],
                attnraw[:, 512 * p:512 * (p + 1)],
                start=(p == 0), stop=False), waits=wmm)
            ev_asq[pr % 2] = EM.op("tensor", lambda e, p=pr, d=sqb: e.matmul(
                pb[6][0:64, :], selq_sb[:, 64 * p:64 * (p + 1)],
                d[:, :], start=False, stop=(p == NCH - 1)),
                waits=[evsq], inc=cT)
            evgn_last = ev_asq[pr % 2]
        ev1 = EM.op("vector", lambda e: e.tensor_scalar(
            gnfin[:, 0:512], pb[6][0:16, :], 1.0 / DH, None, ALU.mult),
            waits=[evgn_last], inc=cV)
        ev2 = EM.op("vector", lambda e: e.tensor_scalar(
            gnfin[:, 512:1024], pb[6][32:48, :], 1.0 / DH, EPS,
            ALU.mult, ALU.add), waits=[evgn_last], inc=cV)
        st_["pb6"] = ev2
        ev3 = EM.op("vector", lambda e: e.scalar_tensor_tensor(
            gnsc[:, :], gnfin[:, 0:512], -1.0, gnfin[:, 0:512],
            ALU.mult, ALU.mult), waits=[ev1], inc=cV)
        ev4 = EM.op("vector", lambda e: e.tensor_tensor(
            gnfin[:, 512:1024], gnfin[:, 512:1024], gnsc[:, :], ALU.add),
            waits=[ev2, ev3], inc=cV)
        ev5 = EM.op("vector", lambda e: e.reciprocal(
            gnsc[:, :], gnfin[:, 512:1024]), waits=[ev4], inc=cV)
        ev6 = EM.op("scalar", lambda e: e.activation(
            gnfin[:, 512:1024], gnsc[:, :], AF_.Sqrt), waits=[ev5], inc=cS)
        ev7 = EM.op("vector", lambda e: e.scalar_tensor_tensor(
            gnfin[:, 0:512], gnfin[:, 0:512], -1.0, gnfin[:, 512:1024],
            ALU.mult, ALU.mult), waits=[ev6], inc=cV)
        ev_prev_nrm = None
        ev_nrm_last = None
        for pr in range(NCH):
            wbc = [ev7, ev_prev_nrm]
            if pr == 0:
                wbc += [st_["pb4"], st_["pb5"]]
            EM.op("tensor", lambda e, p=pr: e.matmul(
                pb[4][:, :], selp_sb[:, 128 * p:128 * (p + 1)],
                gnfin[:, 512:1024], start=True, stop=True), waits=wbc)
            evbg = EM.op("tensor", lambda e, p=pr: e.matmul(
                pb[5][:, :], selp_sb[:, 128 * p:128 * (p + 1)],
                gnfin[:, 0:512], start=True, stop=True), inc=cT)
            EM.op("vector", lambda e, p=pr: e.tensor_tensor(
                ttmp[:, :], attnraw[:, 512 * p:512 * (p + 1)], pb[4][:, :],
                ALU.mult), waits=[evbg])
            evt2 = EM.op("vector", lambda e: e.tensor_tensor(
                ttmp[:, :], ttmp[:, :], pb[5][:, :], ALU.add), inc=cV)
            ev_prev_nrm = evt2
            ev_nrm_last = EM.op("vector", lambda e, p=pr: e.tensor_scalar(
                attnN[:, 512 * p:512 * (p + 1)], ttmp[:, :],
                gnw_sb[:, p:p + 1], gnb_sb[:, p:p + 1], ALU.mult, ALU.add),
                waits=[evt2, ev_attnN_free], inc=cV)
        st_["pb4"] = ev_prev_nrm
        st_["pb5"] = ev_prev_nrm

        if "attnN" in tap_out:
            for cc in range(NCH):
                EM.op("gpsimd", lambda e, c=cc, b=qb: e.dma_start(
                    out=tap_out["attnN"][128 * c:128 * (c + 1),
                                         512 * b:512 * (b + 1)],
                    in_=attnN[:, 512 * c:512 * (c + 1)]),
                    waits=allevs(), inc=dG)

        ev_af = {}
        ev_wo_last = None
        for oc in range(NCH):
            par = oc % 2
            w = [ev_wo, ev_nrm_last] + ev_sco[par]
            evmm = None
            for cc in range(NCH):
                evmm = EM.op("tensor", lambda e, c=cc, o=oc, p=par,
                             wb=wbuf: e.matmul(
                    pb[p][:, :],
                    wb[:, c * 1024 + o * 128:c * 1024 + o * 128 + 128],
                    attnN[:, 512 * c:512 * (c + 1)],
                    start=(c == 0), stop=(c == NCH - 1)),
                    waits=(w if cc == 0 else []),
                    inc=(cT if cc == NCH - 1 else None))
            ev_wo_last = evmm
            wev = [evmm]
            if oc <= 1:
                wev += [ev_af_free, ev_x2dma]
            ev_af[oc] = EM.op("scalar", lambda e, o=oc, p=par: e.copy(
                af[:, 512 * o:512 * (o + 1)], pb[p][:, :]), waits=wev, inc=cS)
            ev_sco[par] = [ev_af[oc]]
        ev_attnN_free = ev_wo_last
        ln_sections.append((qb, dict(ev_af)))

        if "af" in tap_out:
            for cc in range(NCH):
                EM.op("gpsimd", lambda e, c=cc, b=qb: e.dma_start(
                    out=tap_out["af"][128 * c:128 * (c + 1),
                                      512 * b:512 * (b + 1)],
                    in_=af[:, 512 * c:512 * (c + 1)]), waits=allevs(), inc=dG)

        if qb == NQB - 1:
            emit_ln_section(qb, dict(ev_af))

    if "x2" in tap_out:
        EM.op("gpsimd", lambda e: e.dma_start(
            out=tap_out["x2"][:, :], in_=x2_hbm[:, :]), waits=allevs(), inc=dG)

    if phase_limit == "p3":
        return finish(attn)

    barrier_evs = allevs()
    attn.close()

    ffn = ExitStack()
    racc = sb(ffn, "racc", [128, NCH * T], F32)
    xln2_bf = sb(ffn, "xln2_bf", [128, NCH * T], BF16)
    rw_sb = sb(ffn, "rw_sb", [128, NCH * E], BF16)
    rb_sb = sb(ffn, "rb_sb", [1, E], F32)
    b1_sb = sb(ffn, "b1_sb", [128, 2 * E], F32)
    sb1_sb = sb(ffn, "sb1_sb", [128, 32], F32)
    b2_sb = sb(ffn, "b2_sb", [64, C], BF16)
    sb2_sb = sb(ffn, "sb2_sb", [2, C], F32)
    sb2f_sb = sb(ffn, "sb2f_sb", [128, 16], F32)
    exps_sb = sb(ffn, "exps_sb", [128, 2 * E], F32)
    t8_sb = sb(ffn, "t8_sb", [128, 8], F32)
    cwun_sb = sb(ffn, "cwun_sb", [128, 16 * E], BF16)
    nmax_sb = sb(ffn, "nmax_sb", [128, 16], F32)
    dsum_sb = sb(ffn, "dsum_sb", [128, 16], F32)
    rden_sb = sb(ffn, "rden_sb", [128, 16], F32)
    cwf_sb = sb(ffn, "cwf_sb", [128, 2 * E], F32)
    id64_sb = sb(ffn, "id64_sb", [64, 64], BF16)
    cw_em_bf = sb(ffn, "cw_em_bf", [64, T], BF16)
    moe = ExitStack()
    slot_cols = VPS * 4096
    wslot = sb(moe, "wslot", [128, NSLOT * slot_cols], BF16)
    g_all = sb(moe, "g_all", [128, 2 * VPS * 1024], BF16)

    ev_xln2_ld = None
    for cc in range(NCH):
        ev_xln2_ld = EM.op("sync", lambda e, c=cc: e.dma_start(
            out=xln2_bf[:, c * T:(c + 1) * T],
            in_=xln2_hbm[128 * c:128 * (c + 1), :]),
            waits=(barrier_evs if cc == 0 else []), inc=dS)
    if "xln2" in tap_out:
        EM.op("gpsimd", lambda e: e.dma_start(
            out=tap_out["xln2"][:, :], in_=xln2_hbm[:, :]),
            waits=allevs(), inc=dG)

    EM.op("gpsimd", lambda e: e.dma_start(
        out=rw_sb[:, :].rearrange("p (c e) -> p c e", c=NCH),
        in_=rw[:, :].rearrange("(c p) e -> p c e", p=128)),
        waits=barrier_evs, inc=dG)
    for dst, src in ((rb_sb, rb), (b1_sb, b1L), (sb1_sb, sb1L),
                     (sb2_sb, sb2d)):
        EM.op("sync", lambda e, d=dst, s=src: e.dma_start(
            out=d[:, :], in_=s[:, :]), waits=barrier_evs, inc=dS)
    for dst, src in ((b2_sb, b2d), (id64_sb, c_id64)):
        EM.op("gpsimd", lambda e, d=dst, s=src: e.dma_start(
            out=d[:, :], in_=s[:, :]), waits=barrier_evs, inc=dG)
    ev_ffn_pre = [dS, dG]

    for dc in range(NCH):
        EM.op("tensor", lambda e, d=dc: e.matmul(
            pb[6][:, 2 * d:2 * d + 2], sb2_sb[0:2, 128 * d:128 * (d + 1)],
            ones22_sb[:, :], start=(d == 0), stop=(d == NCH - 1)),
            waits=(ev_ffn_pre + [st_["pb6"]] if dc == 0 else []),
            inc=(cT if dc == NCH - 1 else None))
    ev_sb2mm = (cT.sem, cT.n)
    ev_sb2f = EM.op("scalar", lambda e: e.copy(
        sb2f_sb[:, :], pb[6][:, 0:16]), waits=[ev_sb2mm], inc=cS)
    st_["pb6"] = ev_sb2f

    ev_cwf_free = [None, None]
    for i in range(16):
        w = [ev_xln2_ld, st_["pb6"]] + ev_ffn_pre
        for cc in range(NCH):
            EM.op("tensor", lambda e, c=cc, ii=i: e.matmul(
                pb[6][:, 0:64],
                xln2_bf[:, c * T + 128 * ii:c * T + 128 * (ii + 1)],
                rw_sb[:, c * E:(c + 1) * E],
                start=(c == 0), stop=False), waits=(w if cc == 0 else []))
        evl = EM.op("tensor", lambda e: e.matmul(
            pb[6][:, 0:64], ones1_sb[:, :], rb_sb[:, :],
            start=False, stop=True), inc=cT)
        evnm = EM.op("vector", lambda e, ii=i: e.reduce_max(
            nmax_sb[:, ii:ii + 1], pb[6][:, 0:64], axis=AX.X, negate=True),
            waits=[evl], inc=cV)
        evex = EM.op("scalar", lambda e, ii=i: e.activation(
            exps_sb[:, (ii % 2) * E:(ii % 2 + 1) * E], pb[6][:, 0:64],
            AF_.Exp, bias=nmax_sb[:, ii:ii + 1]), waits=[evnm], inc=cS)
        st_["pb6"] = evex
        evt8 = EM.op("vector", lambda e, ii=i: e.max(
            out=t8_sb[:, :], in_=exps_sb[:, (ii % 2) * E:(ii % 2 + 1) * E]),
            waits=[evex], inc=cV)
        evcw = EM.op("vector", lambda e, ii=i: e.scalar_tensor_tensor(
            cwun_sb[:, ii * E:(ii + 1) * E],
            exps_sb[:, (ii % 2) * E:(ii % 2 + 1) * E], t8_sb[:, 7:8],
            exps_sb[:, (ii % 2) * E:(ii % 2 + 1) * E],
            ALU.is_ge, ALU.mult), waits=[evt8], inc=cV)
        ev_ds = EM.op("vector", lambda e, ii=i: e.reduce_sum(
            dsum_sb[:, ii:ii + 1], cwun_sb[:, ii * E:(ii + 1) * E], axis=AX.X),
            waits=[evcw], inc=cV)
    evrd = EM.op("vector", lambda e: e.reciprocal(rden_sb[:, :], dsum_sb[:, :]),
                 waits=[ev_ds], inc=cV)
    ev_cwem = None
    for i in range(16):
        evcf = EM.op("vector", lambda e, ii=i: e.tensor_scalar(
            cwf_sb[:, (ii % 2) * E:(ii % 2 + 1) * E],
            cwun_sb[:, ii * E:(ii + 1) * E], rden_sb[:, ii:ii + 1], None,
            ALU.mult), waits=[evrd, ev_cwf_free[i % 2]], inc=cV)
        evtr = EM.op("tensor", lambda e, ii=i: e.transpose(
            pb[7][0:64, 0:128], cwf_sb[:, (ii % 2) * E:(ii % 2 + 1) * E],
            ident_sb[:, :]), waits=[evcf, st_["pb7"]], inc=cT)
        ev_cwf_free[i % 2] = evtr
        ev_cwem = EM.op("scalar", lambda e, ii=i: e.copy(
            cw_em_bf[:, 128 * ii:128 * (ii + 1)], pb[7][0:64, 0:128]),
            waits=[evtr], inc=cS)
        st_["pb7"] = ev_cwem
    if "cwem" in tap_out:
        EM.op("gpsimd", lambda e: e.dma_start(
            out=tap_out["cwem"][:, :], in_=cw_em_bf[0:64, :]),
            waits=allevs(), inc=dG)
    if "cwun" in tap_out:
        EM.op("gpsimd", lambda e: e.dma_start(
            out=tap_out["cwun"][:, :], in_=cwun_sb[:, :]),
            waits=allevs(), inc=dG)
    if "rdbg" in tap_out:
        for off, buf in ((0, nmax_sb), (16, dsum_sb), (32, rden_sb)):
            EM.op("gpsimd", lambda e, o=off, b=buf: e.dma_start(
                out=tap_out["rdbg"][:, o:o + 16], in_=b[:, :]),
                waits=allevs(), inc=dG)

    if phase_limit == "p4":
        return finish(moe, ffn)

    vx = []
    for s in range(ES):
        for half in range(2):
            for j4 in range(4):
                vx.append(("sh", s, half, j4))
    for e_ in range(E):
        vx.append(("rt", e_))
    ngroups = (len(vx) + VPS - 1) // VPS
    groups = [vx[g * VPS:(g + 1) * VPS] for g in range(ngroups)]
    nrounds = (ngroups + 1) // 2
    last_routed_round = nrounds - 1

    def ve_dma(ve, slot, vi):
        base = slot * slot_cols + vi * 4096
        if ve[0] == "rt":
            e_ = ve[1]
            f1 = lambda e: e.dma_start(
                out=wslot[:, base:base + 2048].rearrange(
                    "p (c h) -> p c h", c=NCH),
                in_=w1[e_].rearrange("(c p) h -> p c h", p=128))
            f2 = lambda e: e.dma_start(
                out=wslot[:, base + 2048:base + 4096].rearrange(
                    "p (c d) -> p c d", c=2),
                in_=w2[e_].rearrange("(c p) d -> p c d", p=128))
        else:
            _, s, half, j4 = ve
            f1 = lambda e: e.dma_start(
                out=wslot[:, base:base + 2048].rearrange(
                    "p (c h) -> p c h", c=NCH),
                in_=sw1h[s, half, :, j4 * 256:(j4 + 1) * 256].rearrange(
                    "(c p) h -> p c h", p=128))
            f2 = lambda e: e.dma_start(
                out=wslot[:, base + 2048:base + 4096].rearrange(
                    "p (c d) -> p c d", c=2),
                in_=sw2[s][half * 1024 + j4 * 256:half * 1024 + (j4 + 1) * 256,
                           :].rearrange("(c p) d -> p c d", p=128))
        return f1, f2

    ev_slot_dma = {}
    ev_slot_freed = {}

    def emit_group_dma(g):
        slot = g % NSLOT
        w = list(barrier_evs)
        if g >= NSLOT:
            w.append(ev_slot_freed[g - NSLOT])
        for vi, ve in enumerate(groups[g]):
            f1, f2 = ve_dma(ve, slot, vi)
            EM.op("gpsimd", f1, waits=(w if vi == 0 else []), inc=dWs[slot])
            EM.op("gpsimd", f2, inc=dWs[slot])
        ev_slot_dma[g] = dWs[slot]

    for g in range(min(NSLOT, ngroups)):
        emit_group_dma(g)

    def b1ap(ve, hfc):
        if ve[0] == "rt":
            cc = ve[1] * 2 + hfc
            return b1_sb[:, cc:cc + 1]
        _, s, half, j4 = ve
        cc = s * 16 + half * 8 + j4 * 2 + hfc
        return sb1_sb[:, cc:cc + 1]

    ev_router_done = [(cV.sem, cV.n), (cS.sem, cS.n), (cT.sem, cT.n)]
    ev_hps = [list(barrier_evs) + [st_["pb6"]], list(barrier_evs)]
    ev_cwps = [list(barrier_evs), list(barrier_evs)]
    ev_flush = [ev_router_done, ev_router_done,
                ev_router_done + [st_["pb6"]], ev_router_done + [st_["pb7"]]]
    ev_g_free = [None] * (2 * VPS)
    ev_mm2 = None
    nrnd = (min(nrounds, int(os.environ.get("KRND", "99")))
            if phase_limit == "p5" else nrounds)
    sk5 = (os.environ.get("KSKIP5", "").split(",")
           if phase_limit == "p5" else [])
    for ri in range(nrnd):
        ves = []
        for g in (2 * ri, 2 * ri + 1):
            if g < ngroups:
                for vi, ve in enumerate(groups[g]):
                    ves.append((g, vi, ve))
        is_last = (ri == last_routed_round)
        for tc in range(NQB):
            ev_gready = {}
            for xi, (g, vi, ve) in enumerate(ves):
                par = xi % 2
                slot = g % NSLOT
                base = slot * slot_cols + vi * 4096
                routed = ve[0] == "rt"
                evcwb = None
                if routed:
                    evcwb = EM.op("tensor", lambda e, e_=ve[1], t_=tc, p=par:
                                  e.matmul(
                        pb[2 + p][:, :],
                        bcast_col(id64_sb[:, e_:e_ + 1], 128),
                        cw_em_bf[0:64, 512 * t_:512 * (t_ + 1)],
                        start=True, stop=True),
                        waits=[ev_cwem] + ev_cwps[par], inc=cT)
                for hfc in range(2):
                    w = [ev_slot_dma[g], ev_xln2_ld] + ev_hps[par]
                    if "mm1" in sk5:
                        ev = EM.op("tensor", lambda e: e.nop(), waits=w, inc=cT)
                        ev_hps[par] = [ev]
                        ev_gready[(xi, hfc)] = ev
                        continue
                    evh = None
                    for cc in range(NCH):
                        evh = EM.op("tensor", lambda e, c=cc, hf=hfc, b_=base,
                                    t_=tc, p=par: e.matmul(
                            pb[p][:, :],
                            wslot[:, b_ + c * 256 + hf * 128:
                                  b_ + c * 256 + hf * 128 + 128],
                            xln2_bf[:, c * T + 512 * t_:c * T + 512 * (t_ + 1)],
                            start=(c == 0), stop=(c == NCH - 1)),
                            waits=(w if cc == 0 else []),
                            inc=(cT if cc == NCH - 1 else None))
                    gslot = xi
                    wsl = [evh]
                    if hfc == 0:
                        wsl.append(ev_g_free[gslot])
                    if routed:
                        evs_ = EM.op("scalar", lambda e, hf=hfc, gs=gslot,
                                     p=par, v=ve: e.activation(
                            g_all[:, gs * 1024 + 512 * hf:
                                  gs * 1024 + 512 * (hf + 1)],
                            pb[p][:, :],
                            AF_.Silu, bias=b1ap(v, hf)), waits=wsl, inc=cS)
                        ev_hps[par] = [evs_]
                        evg = EM.op("vector", lambda e, hf=hfc, gs=gslot, p=par:
                                    e.tensor_tensor(
                            g_all[:, gs * 1024 + 512 * hf:
                                  gs * 1024 + 512 * (hf + 1)],
                            g_all[:, gs * 1024 + 512 * hf:
                                  gs * 1024 + 512 * (hf + 1)],
                            pb[2 + p][:, :], ALU.mult),
                            waits=[evs_, evcwb], inc=cV)
                        ev_cwps[par] = [evg]
                        ev_gready[(xi, hfc)] = evg
                    else:
                        evs_ = EM.op("scalar", lambda e, hf=hfc, gs=gslot,
                                     p=par, v=ve: e.activation(
                            g_all[:, gs * 1024 + 512 * hf:
                                  gs * 1024 + 512 * (hf + 1)],
                            pb[p][:, :], AF_.Silu, bias=b1ap(v, hf)),
                            waits=wsl, inc=cS)
                        ev_hps[par] = [evs_]
                        ev_gready[(xi, hfc)] = evs_
            for half in range(2):
                nves = len(ves)
                if "mm2" in sk5:
                    ev_mm2 = EM.op("tensor", lambda e: e.nop(),
                                   waits=[ev_gready[(xi2, 1)]
                                          for xi2 in range(nves)], inc=cT)
                    continue
                for xi, (g, vi, ve) in enumerate(ves):
                    slot = g % NSLOT
                    base = slot * slot_cols + vi * 4096
                    final = (xi == nves - 1)
                    for hfc in range(2):
                        for dc in range(NCH):
                            stt = (xi == 0 and hfc == 0 and dc % 2 == 0)
                            spp = (not is_last and final
                                   and hfc == 1 and dc % 2 == 1)
                            w = [ev_gready[(xi, hfc)]]
                            if stt:
                                w.append(ev_flush[dc // 2])
                            lastmm = (not is_last and final and hfc == 1
                                      and dc == NCH - 1)
                            ev = EM.op("tensor", lambda e, hf=hfc, d=dc,
                                       b_=base, gs=xi, h_=half, st=stt, sp=spp:
                                       e.matmul(
                                pb[4 + d // 2][:, (d % 2) * 256:
                                               (d % 2) * 256 + 256],
                                wslot[:, b_ + 2048 + hf * 1024 + d * 128:
                                      b_ + 2048 + hf * 1024 + d * 128 + 128],
                                g_all[:, gs * 1024 + hf * 512 + h_ * 256:
                                      gs * 1024 + hf * 512 + h_ * 256 + 256],
                                start=st, stop=sp), waits=w,
                                inc=(cT if lastmm else None))
                            if lastmm:
                                ev_mm2 = ev
                if is_last:
                    for dc in range(NCH):
                        ev_mm2 = EM.op("tensor", lambda e, d=dc, t_=tc, h_=half:
                                       e.matmul(
                            pb[4 + d // 2][:, (d % 2) * 256:(d % 2) * 256 + 256],
                            b2_sb[0:64, d * 128:(d + 1) * 128],
                            cw_em_bf[0:64, 512 * t_ + 256 * h_:
                                     512 * t_ + 256 * h_ + 256],
                            start=False, stop=(d % 2 == 1)), waits=[ev_cwem],
                            inc=(cT if dc == NCH - 1 else None))
                for jjx in range(4):
                    off = 512 * tc + 256 * half
                    outap = racc[:, :].rearrange(
                        "p (c t) -> p c t", c=NCH)[:, 2 * jjx:2 * jjx + 2,
                                                   off:off + 256]
                    inap = pb[4 + jjx][:, :].rearrange("p (r t) -> p r t", r=2)
                    if "flush" in sk5:
                        fn_ = (lambda e: e.engine_nop())
                    elif ri == 0:
                        fn_ = (lambda e, oa=outap, ia=inap:
                               e.tensor_copy(oa, ia))
                    else:
                        fn_ = (lambda e, oa=outap, ia=inap:
                               e.tensor_tensor(oa, oa, ia, ALU.add))
                    ev_flush[jjx] = EM.op("vector", fn_, waits=[ev_mm2], inc=cV)
            for xi in range(len(ves)):
                ev_g_free[xi] = ev_mm2
        for g in (2 * ri, 2 * ri + 1):
            if g < ngroups:
                ev_slot_freed[g] = ev_mm2
                if g + NSLOT < ngroups:
                    emit_group_dma(g + NSLOT)

    if "racc" in tap_out:
        for cc in range(NCH):
            EM.op("gpsimd", lambda e, c=cc: e.dma_start(
                out=tap_out["racc"][128 * c:128 * (c + 1), :],
                in_=racc[:, c * T:(c + 1) * T]), waits=allevs(), inc=dG)

    if phase_limit == "p5":
        return finish(moe, ffn)

    moe_bar = allevs()
    moe.close()

    epi = ExitStack()
    up = sb(epi, "up", [128, NCH * 512], F32)
    x2t = [sb(epi, "x2t0", [128, 512], F32), sb(epi, "x2t1", [128, 512], F32)]
    outb = [sb(epi, "outb0", [128, 512], F32), sb(epi, "outb1", [128, 512], F32)]
    xsq_e = [sb(epi, "xsqe0", [128, 512], F32), sb(epi, "xsqe1", [128, 512], F32)]
    ttmp_e = sb(epi, "ttmpe", [128, 512], F32)
    ev_xsqf_e = [moe_bar, moe_bar]
    st_["pb4"] = ev_flush[0]
    st_["pb5"] = ev_flush[1]
    st_["pb6"] = ev_flush[2]
    st_["pb7"] = ev_flush[3]
    ev_x2t = [moe_bar, moe_bar]
    ev_outb = [None, None]
    for tc in range(NQB):
        ev_u = {}
        for cc in range(NCH):
            w = [ev_sb2f] + ev_flush
            if tc == 0 and cc == 0:
                w = w + moe_bar
            ev_u[cc] = EM.op("vector", lambda e, c=cc, t_=tc: e.tensor_scalar(
                up[:, 512 * c:512 * (c + 1)],
                racc[:, c * T + 512 * t_:c * T + 512 * (t_ + 1)],
                sb2f_sb[:, 2 * c:2 * c + 1], None, ALU.add),
                waits=w, inc=cV)
        evb = ln_block(lambda cc: up[:, 512 * cc:512 * (cc + 1)],
                       lambda cc: ev_u[cc], xsq_e, ev_xsqf_e)
        evtt = None
        for cc in range(NCH):
            par = cc % 2
            evx = EM.op("sync", lambda e, c=cc, t_=tc, p=par: e.dma_start(
                out=x2t[p][:, :],
                in_=x2_hbm[128 * c:128 * (c + 1), 512 * t_:512 * (t_ + 1)]),
                waits=[ev_x2t[par]], inc=dS)
            EM.op("vector", lambda e, c=cc: e.tensor_tensor(
                ttmp_e[:, :], up[:, 512 * c:512 * (c + 1)], pb[4][:, :],
                ALU.mult), waits=[evb])
            EM.op("vector", lambda e: e.tensor_tensor(
                ttmp_e[:, :], ttmp_e[:, :], pb[5][:, :], ALU.add))
            evtt = EM.op("vector", lambda e, p=par: e.tensor_tensor(
                outb[p][:, :], ttmp_e[:, :], x2t[p][:, :], ALU.add),
                waits=[evx, ev_outb[par]], inc=cV)
            ev_x2t[par] = evtt
            ev_outb[par] = EM.op("sync", lambda e, c=cc, t_=tc, p=par:
                                 e.dma_start(
                out=out_fm[128 * c:128 * (c + 1), 512 * t_:512 * (t_ + 1)],
                in_=outb[p][:, :]), waits=[evtt], inc=dS)
        st_["pb4"] = evtt
        st_["pb5"] = evtt

    return finish(epi, ffn)


def host_inputs(inputs):
    asf = lambda a: np.ascontiguousarray(np.asarray(a), dtype=np.float32)
    Wq = asf(inputs["L0_attn_Wq"]); Wk = asf(inputs["L0_attn_Wk"])
    Wv = asf(inputs["L0_attn_Wv"]); Wo = asf(inputs["L0_attn_Wo"])
    b1 = asf(inputs["L0_ffn_b1"]); sb1 = asf(inputs["L0_ffn_sb1"])
    sw1 = asf(inputs["L0_ffn_sw1"])
    return {
        "wqT": np.ascontiguousarray(Wq.T), "wkT": np.ascontiguousarray(Wk.T),
        "wvT": np.ascontiguousarray(Wv.T), "woT": np.ascontiguousarray(Wo.T),
        "gnw8": np.ascontiguousarray(
            asf(inputs["L0_attn_gn_weight"]).reshape(8, 128).T),
        "gnb8": np.ascontiguousarray(
            asf(inputs["L0_attn_gn_bias"]).reshape(8, 128).T),
        "rw": asf(inputs["L0_ffn_router_w"]),
        "rb": asf(inputs["L0_ffn_router_b"]).reshape(1, E),
        "w1": asf(inputs["L0_ffn_w1"]),
        "b1L": np.ascontiguousarray(
            b1.reshape(E, 2, 128).transpose(2, 0, 1).reshape(128, 2 * E)),
        "w2": asf(inputs["L0_ffn_w2"]),
        "b2d": asf(inputs["L0_ffn_b2"]),
        "sw1h": np.ascontiguousarray(
            sw1.reshape(ES, C, 2, 1024).transpose(0, 2, 1, 3)),
        "sb1L": np.ascontiguousarray(
            sb1.reshape(ES, 16, 128).transpose(2, 0, 1).reshape(128, 32)),
        "sw2": asf(inputs["L0_ffn_sw2"]),
        "sb2d": asf(inputs["L0_ffn_sb2"]),
    }


def kernel(**inputs):
    xs = np.ascontiguousarray(np.asarray(inputs["L0_attn_x"]), dtype=np.float32)
    shared = host_inputs(inputs)
    nc = build_program()
    in_maps = []
    for b in range(B):
        m = dict(shared)
        m["xfm"] = np.ascontiguousarray(xs[b].T)
        in_maps.append(m)
    res = run_bass_kernel_spmd(nc, in_maps, core_ids=list(range(B)))
    out = np.empty((B, T, C), dtype=np.float32)
    for b in range(B):
        out[b] = res.results[b]["out_fm"].T
    return out
